# version 04
# baseline (speedup 1.0000x reference)
import contextlib
import numpy as np
import concourse.bass as bass
import concourse.mybir as mybir
from concourse.bass_utils import run_bass_kernel_spmd
from concourse.alu_op_type import AluOpType

F32 = mybir.dt.float32
AF = mybir.ActivationFunctionType
AX = mybir.AxisListType.X
ADD, SUB, MUL = AluOpType.add, AluOpType.subtract, AluOpType.mult

D = 2048
DC = 16
TC = 1024
S = 4096
NE = 16
FI = 768
FC = 6
SCALE = float(1.0 / np.sqrt(np.float32(128.0)))
LN_EPS = 1e-5
RMS_EPS = 1.1920928955078125e-07
LAMBDA_INIT = 0.8

ENGS = ("te", "ve", "se", "sy", "gp")
DEBUG_TAPS = False
PHASE_LIMIT = 99
FFN_NE = 16
FFN_GATE = True
FFN_DOWN = 3


class Sched:

    def __init__(self, nc):
        self.nc = nc
        self.ops = {e: [] for e in ENGS}
        self.sem = {}
        self.cnt = {}

    def mksem(self, stack, *names):
        for n in names:
            self.sem[n] = stack.enter_context(self.nc.semaphore(n))
            self.cnt[n] = 0

    def wait(self, eng, sname, val=None):
        v = self.cnt[sname] if val is None else val
        if v <= 0:
            return
        s = self.sem[sname]
        self.ops[eng].append(lambda E: E.wait_ge(s, v))

    def op(self, eng, fn, inc=None, inck=1, drain=False):
        if inc is None:
            self.ops[eng].append(fn)
        else:
            s = self.sem[inc]
            self.ops[eng].append(lambda E: fn(E).then_inc(s, inck))
            self.cnt[inc] += inck
        if drain:
            self.ops[eng].append(lambda E: E.drain())
        return self.cnt[inc] if inc is not None else None


class _PhaseStop(Exception):
    pass


def ap3(h, off, dims):
    t = getattr(h, "tensor", h)
    return bass.AP(t, off, dims)


def build_program(phase_limit=None):
    PL = PHASE_LIMIT if phase_limit is None else phase_limit
    nc = bass.Bass("TRN2", target_bir_lowering=False,
                   detect_race_conditions=False)

    din = {}
    for name, shape in [
            ("xT", [DC, 128, TC]), ("rwT", [DC, 128, NE]),
            ("gu", [NE, D, 2 * FI]), ("dwn", [NE, FI, D]),
            ("wq4", [DC, 128, 512]), ("wk4", [DC, 128, 512]),
            ("wv4", [DC, 128, 512]), ("woT", [DC, 128, D]),
            ("lq1", [128, 1]), ("lq2", [128, 1]), ("lk1", [128, 1]),
            ("lk2", [128, 1]), ("id128", [128, 128]), ("ones128", [128, 1]),
            ("onesr", [1, 128]), ("ones64a", [128, 1]), ("ones64b", [128, 1]),
            ("epsln", [128, 1]), ("epsrms", [128, 1])]:
        din[name] = nc.dram_tensor(name, shape, F32, kind="ExternalInput")
    outT = nc.dram_tensor("outT", [DC, 128, TC], F32, kind="ExternalOutput")
    dbg = {}
    if DEBUG_TAPS:
        for nm, shape in (("dbg_y1", [D, TC]), ("dbg_xg", [4, D, TC]),
                          ("dbg_xln", [4, D, TC]), ("dbg_x2", [D, TC]),
                          ("dbg_cmb", [NE, TC]), ("dbg_acc", [D, TC]),
                          ("dbg_xln1", [D, TC]), ("dbg_h", [128, 6144]),
                          ("dbg_xa", [D, TC]), ("dbg_acc0", [D, TC])):
            dbg[nm] = nc.dram_tensor(nm, shape, F32, kind="ExternalOutput")

    y1b = nc.dram_tensor("y1b", [D, TC], F32)
    xg = nc.dram_tensor("xg", [4, D, TC], F32)
    xlnD = nc.dram_tensor("xlnD", [4, D, TC], F32)
    x2b = nc.dram_tensor("x2b", [D, TC], F32)
    cmbD = nc.dram_tensor("cmbD", [NE, TC], F32)

    stack = contextlib.ExitStack()
    with stack:
        sbuf = lambda name, w: stack.enter_context(
            nc.sbuf_tensor(name, [128, w], F32))
        A1 = sbuf("A1", 16384)
        A2 = sbuf("A2", 16384)
        A3 = sbuf("A3", 8192)
        A4 = sbuf("A4", 6144)
        A5 = sbuf("A5", 1536)
        A6 = sbuf("A6", 2688)
        B = [stack.enter_context(nc.psum_tensor(f"B{i}", [128, 512], F32))
             for i in range(8)]

        c = Sched(nc)
        c.mksem(stack, "te", "ve", "se", "dw", "dg", "cc",
                "g0", "g1", "g2", "d0", "d1", "cr",
                "q0", "q1", "qw", "w0", "w1", "r0", "r1",
                "l0", "l1", "s0", "s1")

        O_ID = 0
        O_ONES = 128
        O_O64A = 129
        O_O64B = 130
        O_ONR = 131
        O_LAM = 259
        O_NC1 = 260
        O_NC2 = 261
        O_LQ = 262
        O_QN = 266
        O_EPS1 = 336
        O_EPS2 = 337
        O_SML = 274
        O_ROW = 338
        O_ROW2 = 850
        O_CROW = 1362
        O_RW = 2386

        ONESC = A6[:, O_ONES:O_ONES + 1]
        ONR = A6[:1, O_ONR:O_ONR + 128]
        IDT = A6[:, O_ID:O_ID + 128]

        SQ0, SQ1 = 0, 512
        SILU0, SILU1 = 6144, 6656
        QOF, KOF, OOF, EOF, VST = 0, 4096, 8192, 12288, 14336

        c.op("sy", lambda E: E.dma_start(out=IDT, in_=din["id128"].ap()),
             inc="dw", inck=16)
        c.op("sy", lambda E: E.dma_start(out=ONESC, in_=din["ones128"].ap()),
             inc="dw", inck=16)
        c.op("sy", lambda E: E.dma_start(out=A6[:, O_O64A:O_O64A + 1],
                                         in_=din["ones64a"].ap()),
             inc="dw", inck=16)
        c.op("sy", lambda E: E.dma_start(out=A6[:, O_O64B:O_O64B + 1],
                                         in_=din["ones64b"].ap()),
             inc="dw", inck=16)
        c.op("sy", lambda E: E.dma_start(out=ONR, in_=din["onesr"].ap()),
             inc="dw", inck=16)
        for i, nm in enumerate(("lq1", "lq2", "lk1", "lk2")):
            c.op("sy", lambda E, i=i, nm=nm: E.dma_start(
                out=A6[:, O_LQ + i:O_LQ + i + 1], in_=din[nm].ap()),
                inc="dw", inck=16)
        c.op("sy", lambda E: E.dma_start(
            out=A6[:, O_EPS1:O_EPS1 + 1], in_=din["epsln"].ap()),
             inc="dw", inck=16)
        c.op("sy", lambda E: E.dma_start(
            out=A6[:, O_EPS2:O_EPS2 + 1], in_=din["epsrms"].ap()),
             inc="dw", inck=16)
        c.op("sy", lambda E: E.dma_start(
            out=ap3(A6, O_RW, [[2688, 128], [16, DC], [1, NE]]),
            in_=ap3(din["rwT"], 0, [[NE, 128], [128 * NE, DC], [1, NE]])),
             inc="dw", inck=16)
        c.op("sy", lambda E: E.dma_start(
            out=ap3(A2, 0, [[16384, 128], [TC, DC], [1, TC]]),
            in_=ap3(din["xT"], 0, [[TC, 128], [128 * TC, DC], [1, TC]])),
             inc="dw", inck=16)

        c.wait("te", "dw")
        c.op("te", lambda E: E.matmul(B[6][:1, 0:1], A6[:, O_LQ:O_LQ + 1],
                                      A6[:, O_LQ + 2:O_LQ + 3],
                                      start=True, stop=True))
        c.op("te", lambda E: E.matmul(B[7][:1, 0:1], A6[:, O_LQ + 1:O_LQ + 2],
                                      A6[:, O_LQ + 3:O_LQ + 4],
                                      start=True, stop=True), inc="te")
        c.wait("se", "te")
        c.op("se", lambda E: E.activation(A6[:1, O_SML:O_SML + 1],
                                          B[6][:1, 0:1], AF.Exp))
        c.op("se", lambda E: E.activation(A6[:1, O_SML + 1:O_SML + 2],
                                          B[7][:1, 0:1], AF.Exp), inc="se")
        c.wait("ve", "se")
        c.op("ve", lambda E: E.tensor_sub(A6[:1, O_LAM:O_LAM + 1],
                                          A6[:1, O_SML:O_SML + 1],
                                          A6[:1, O_SML + 1:O_SML + 2]),
             drain=True)
        c.op("ve", lambda E: E.tensor_scalar(A6[:1, O_LAM:O_LAM + 1],
                                             A6[:1, O_LAM:O_LAM + 1],
                                             LAMBDA_INIT, None, ADD),
             inc="ve", drain=True)

        def ln_block(src, dst, tw):
            c.wait("te", "ve")
            c.wait("te", "dw")
            c.wait("se", "ve")
            c.wait("se", "te")
            c.wait("se", "dw")
            for k in range(DC):
                c.op("te", lambda E, k=k: E.matmul(
                    B[5][:1, 0:tw], ONESC, src(k),
                    start=(k == 0), stop=(k == DC - 1)),
                    inc="te" if k == DC - 1 else None)
            te_sq = {}
            for k in range(DC):
                sl = SQ0 if k % 2 == 0 else SQ1
                if k >= 2:
                    c.wait("se", "te", te_sq[k - 2])
                c.op("se", lambda E, k=k, sl=sl: E.activation(
                    A3[:, sl:sl + tw], src(k), AF.Square), inc="se")
                se_k = c.cnt["se"]
                c.wait("te", "se", se_k)
                te_sq[k] = c.op("te", lambda E, k=k, sl=sl: E.matmul(
                    B[6][:1, 0:tw], ONESC, A3[:, sl:sl + tw],
                    start=(k == 0), stop=(k == DC - 1)), inc="te")
            c.wait("ve", "te")
            c.op("ve", lambda E: E.tensor_scalar(
                A6[:1, O_ROW:O_ROW + tw], B[5][:1, 0:tw], 1.0 / D, None,
                MUL), drain=True)
            c.op("ve", lambda E: E.tensor_scalar(
                A6[:1, O_ROW2:O_ROW2 + tw], B[6][:1, 0:tw], 1.0 / D, None,
                MUL), drain=True)
            c.op("ve", lambda E: E.tensor_mul(
                B[6][:1, 0:tw], A6[:1, O_ROW:O_ROW + tw],
                A6[:1, O_ROW:O_ROW + tw]), drain=True)
            c.op("ve", lambda E: E.tensor_sub(
                A6[:1, O_ROW2:O_ROW2 + tw], A6[:1, O_ROW2:O_ROW2 + tw],
                B[6][:1, 0:tw]), inc="ve", drain=True)
            c.wait("se", "ve")
            c.op("se", lambda E: E.activation(
                A6[:1, O_ROW2:O_ROW2 + tw], A6[:1, O_ROW2:O_ROW2 + tw],
                AF.Sqrt, bias=A6[:1, O_EPS1:O_EPS1 + 1]), inc="se")
            c.wait("ve", "se")
            c.op("ve", lambda E: E.reciprocal(
                A6[:1, O_ROW2:O_ROW2 + tw], A6[:1, O_ROW2:O_ROW2 + tw]),
                drain=True)
            c.op("ve", lambda E: E.tensor_mul(
                A6[:1, O_ROW:O_ROW + tw], A6[:1, O_ROW:O_ROW + tw],
                A6[:1, O_ROW2:O_ROW2 + tw]), inc="ve", drain=True)
            c.wait("te", "ve")
            c.op("te", lambda E: E.matmul(B[4][:, 0:tw], ONR,
                                          A6[:1, O_ROW2:O_ROW2 + tw],
                                          start=True, stop=True))
            c.op("te", lambda E: E.matmul(B[5][:, 0:tw], ONR,
                                          A6[:1, O_ROW:O_ROW + tw],
                                          start=True, stop=True), inc="te")
            c.wait("ve", "te")
            for k in range(DC):
                c.op("ve", lambda E, k=k: E.tensor_mul(dst(k), src(k),
                                                       B[4][:, 0:tw]))
                c.op("ve", lambda E, k=k: E.tensor_sub(dst(k), dst(k),
                                                       B[5][:, 0:tw]),
                     inc="ve" if k == DC - 1 else None)

        def router(xln):
            def vop(fn, inc=None, inck=1, drain=True):
                return c.op("ve", fn, inc=inc, inck=inck, drain=True)
            LG, LGT = 0, 512
            PP, RR, SEL, KP = 576, 592, 608, 624
            MM_, NM, SM4, IV = 640, 641, 642, 643
            CMT, CST = 648, 664
            for tb in range(2):
                c.wait("te", "ve")
                c.wait("te", "dw")
                for k in range(DC):
                    c.op("te", lambda E, k=k, tb=tb: E.matmul(
                        B[6][:16, 0:512],
                        A6[:, O_RW + k * 16:O_RW + k * 16 + 16], xln(k, tb),
                        start=(k == 0), stop=(k == DC - 1)),
                        inc="te" if k == DC - 1 else None)
                c.wait("ve", "te")
                vop( lambda E: E.tensor_copy(A5[:16, LG:LG + 512],
                                                   B[6][:16, 0:512]),
                     inc="ve", drain=True)
                for j in range(4):
                    c.wait("te", "ve")
                    c.op("te", lambda E, j=j: E.matmul(
                        B[7][:, 0:16],
                        A5[:16, LG + j * 128:LG + j * 128 + 128],
                        IDT[:16, 0:16], is_transpose=True,
                        start=True, stop=True), inc="te")
                    c.wait("ve", "te")
                    vop( lambda E, j=j: E.tensor_copy(
                        A5[:, LGT + j * 16:LGT + j * 16 + 16], B[7][:, 0:16]),
                        inc="ve", drain=True)
                for j in range(4):
                    vop( lambda E, j=j: E.reduce_max(
                        A5[:, MM_:MM_ + 1],
                        A5[:, LGT + j * 16:LGT + j * 16 + 16], axis=AX))
                    vop( lambda E: E.tensor_scalar(
                        A5[:, NM:NM + 1], A5[:, MM_:MM_ + 1], -1.0, None,
                        MUL), inc="ve", drain=True)
                    c.wait("se", "ve")
                    c.op("se", lambda E, j=j: E.activation(
                        A5[:, PP:PP + 16],
                        A5[:, LGT + j * 16:LGT + j * 16 + 16], AF.Exp,
                        bias=A5[:, NM:NM + 1],
                        accum_out=A5[:, IV:IV + 1]), inc="se", drain=True)
                    c.wait("ve", "se")
                    vop( lambda E: E.reciprocal(A5[:, MM_:MM_ + 1],
                                                      A5[:, IV:IV + 1]))
                    vop( lambda E: E.tensor_scalar(
                        A5[:, PP:PP + 16], A5[:, PP:PP + 16],
                        A5[:, MM_:MM_ + 1], None, MUL))
                    vop( lambda E: E.tensor_copy(A5[:, RR:RR + 16],
                                                       A5[:, PP:PP + 16]))
                    for r in range(4):
                        vop( lambda E: E.reduce_max(
                            A5[:, MM_:MM_ + 1], A5[:, RR:RR + 16], axis=AX))
                        vop( lambda E: E.tensor_scalar(
                            A5[:, KP:KP + 16], A5[:, RR:RR + 16],
                            A5[:, MM_:MM_ + 1], None, AluOpType.is_ge))
                        if r == 0:
                            vop( lambda E: E.tensor_copy(
                                A5[:, SEL:SEL + 16], A5[:, KP:KP + 16]))
                            vop( lambda E: E.tensor_copy(
                                A5[:, SM4:SM4 + 1], A5[:, MM_:MM_ + 1]))
                        else:
                            vop( lambda E: E.tensor_add(
                                A5[:, SEL:SEL + 16], A5[:, SEL:SEL + 16],
                                A5[:, KP:KP + 16]))
                            vop( lambda E: E.tensor_add(
                                A5[:, SM4:SM4 + 1], A5[:, SM4:SM4 + 1],
                                A5[:, MM_:MM_ + 1]))
                        if r < 3:
                            vop( lambda E: E.tensor_scalar(
                                A5[:, KP:KP + 16], A5[:, KP:KP + 16],
                                -1.0, 1.0, MUL, ADD))
                            vop( lambda E: E.tensor_mul(
                                A5[:, RR:RR + 16], A5[:, RR:RR + 16],
                                A5[:, KP:KP + 16]))
                    vop( lambda E: E.reciprocal(A5[:, IV:IV + 1],
                                                      A5[:, SM4:SM4 + 1]))
                    vop( lambda E: E.tensor_mul(
                        A5[:, CMT:CMT + 16], A5[:, PP:PP + 16],
                        A5[:, SEL:SEL + 16]))
                    vop( lambda E: E.tensor_scalar(
                        A5[:, CMT:CMT + 16], A5[:, CMT:CMT + 16],
                        A5[:, IV:IV + 1], None, MUL), inc="ve", drain=True)
                    c.wait("te", "ve")
                    c.op("te", lambda E: E.matmul(
                        B[7][:16, 0:128], A5[:, CMT:CMT + 16], IDT,
                        is_transpose=True, start=True, stop=True), inc="te")
                    c.wait("ve", "te")
                    vop( lambda E, j=j, tb=tb: E.tensor_copy(
                        A2[:16, tb * 512 + j * 128:tb * 512 + j * 128 + 128],
                        B[7][:16, 0:128]), inc="ve", drain=True)
            c.wait("gp", "ve")
            c.op("gp", lambda E: E.dma_start(out=cmbD.ap(),
                                             in_=A2[:16, 0:1024]),
                 inc="dg", inck=16)
            return c.cnt["dg"]

        def ffn_experts(xln, cmb_dg, tap0=None):
            c.wait("sy", "dg", cmb_dg)
            c.wait("sy", "te")
            c.wait("sy", "ve")
            c.wait("ve", "te")
            c.wait("ve", "se")
            c.wait("ve", "dg", cmb_dg)
            c.op("ve", lambda E: E.memset(A2[:], 0.0), inc="ve")
            slot_free = [0, 0, 0]
            dslot_free = [0, 0]
            prev_cb = 0
            for e in range(FFN_NE):
                c.wait("sy", "te", prev_cb)
                c.op("sy", lambda E, e=e: E.dma_start(
                    out=A6[:1, O_CROW:O_CROW + TC], in_=cmbD.ap()[e]),
                    inc="cr", inck=16)
                c.wait("te", "cr")
                c.wait("te", "ve")
                for tb in range(2):
                    c.op("te", lambda E, tb=tb: E.matmul(
                        B[4 + tb][:, 0:512], ONR,
                        A6[:1, O_CROW + tb * 512:O_CROW + tb * 512 + 512],
                        start=True, stop=True), inc="te")
                prev_cb = c.cnt["te"]

                loads = []
                for j in range(FC):
                    loads.append(j * 128)
                    loads.append(FI + j * 128)
                ld_done = [None] * len(loads)

                def issue_gu_load(li, e=e, loads=loads, ld_done=ld_done):
                    off = loads[li]
                    sl = li % 3
                    c.wait("sy", "te", slot_free[sl])
                    c.op("sy", lambda E, e=e, off=off, sl=sl: E.dma_start(
                        out=ap3(A4, sl * 2048,
                                [[6144, 128], [128, DC], [1, 128]]),
                        in_=ap3(din["gu"], e * D * 2 * FI + off,
                                [[2 * FI, 128], [128 * 2 * FI, DC],
                                 [1, 128]])), inc=f"g{sl}", inck=16)
                    ld_done[li] = (f"g{sl}", c.cnt[f"g{sl}"])

                if not FFN_GATE:
                    continue
                for li in range(3):
                    issue_gu_load(li)

                for j in range(FC):
                    gsl = (2 * j) % 3
                    usl = (2 * j + 1) % 3
                    c.wait("te", *ld_done[2 * j])
                    c.wait("te", "se")
                    for tb in range(2):
                        for k in range(DC):
                            c.op("te", lambda E, tb=tb, k=k, gsl=gsl:
                                 E.matmul(
                                     B[0 + tb][:, 0:512],
                                     A4[:, gsl * 2048 + k * 128:
                                        gsl * 2048 + k * 128 + 128],
                                     xln(k, tb),
                                     start=(k == 0), stop=(k == DC - 1)),
                                 inc="te" if k == DC - 1 else None)
                    gate_done = c.cnt["te"]
                    slot_free[gsl] = gate_done
                    if 2 * j + 3 < len(loads):
                        issue_gu_load(2 * j + 3)
                    c.wait("se", "te", gate_done)
                    c.wait("se", "ve")
                    for tb in range(2):
                        so = SILU0 if tb == 0 else SILU1
                        c.op("se", lambda E, tb=tb, so=so: E.activation(
                            A3[:, so:so + 512], B[0 + tb][:, 0:512],
                            AF.Silu), inc="se")
                    silu_done = c.cnt["se"]
                    c.wait("te", *ld_done[2 * j + 1])
                    c.wait("te", "ve")
                    for tb in range(2):
                        for k in range(DC):
                            c.op("te", lambda E, tb=tb, k=k, usl=usl:
                                 E.matmul(
                                     B[2 + tb][:, 0:512],
                                     A4[:, usl * 2048 + k * 128:
                                        usl * 2048 + k * 128 + 128],
                                     xln(k, tb),
                                     start=(k == 0), stop=(k == DC - 1)),
                                 inc="te" if k == DC - 1 else None)
                    up_done = c.cnt["te"]
                    slot_free[usl] = up_done
                    if 2 * j + 4 < len(loads):
                        issue_gu_load(2 * j + 4)
                    c.wait("ve", "se", silu_done)
                    c.wait("ve", "te", up_done)
                    for tb in range(2):
                        hc = j * 1024 + tb * 512
                        so = SILU0 if tb == 0 else SILU1
                        c.op("ve", lambda E, tb=tb, hc=hc, so=so:
                             E.scalar_tensor_tensor(
                                 A3[:, hc:hc + 512], B[2 + tb][:, 0:512],
                                 1.0, A3[:, so:so + 512], MUL, MUL),
                             drain=True)
                        c.op("ve", lambda E, tb=tb, hc=hc: E.tensor_mul(
                            A3[:, hc:hc + 512], A3[:, hc:hc + 512],
                            B[4 + tb][:, 0:512]), inc="ve")
                h_done = c.cnt["ve"]

                def issue_down(dd, e=e):
                    sl = dd % 2
                    c.wait("sy", "te", dslot_free[sl])
                    c.op("sy", lambda E, e=e, dd=dd, sl=sl: E.dma_start(
                        out=ap3(A5, sl * 768,
                                [[1536, 128], [128, FC], [1, 128]]),
                        in_=ap3(din["dwn"], e * FI * D + dd * 128,
                                [[D, 128], [128 * D, FC], [1, 128]])),
                        inc=f"d{sl}", inck=16)

                if FFN_DOWN >= 1:
                    issue_down(0)
                    issue_down(1)
                for dd in range(DC if FFN_DOWN >= 1 else 0):
                    sl = dd % 2
                    if FFN_DOWN < 2:
                        if dd + 2 < DC:
                            issue_down(dd + 2)
                        continue
                    c.wait("te", f"d{sl}")
                    if dd == 0:
                        c.wait("te", "ve", h_done)
                    c.wait("te", "ve")
                    for tb in range(2):
                        bk = 6 + tb
                        for f in range(FC):
                            c.op("te", lambda E, tb=tb, f=f, sl=sl, bk=bk:
                                 E.matmul(
                                     B[bk][:, 0:512],
                                     A5[:, sl * 768 + f * 128:
                                        sl * 768 + f * 128 + 128],
                                     A3[:, f * 1024 + tb * 512:
                                        f * 1024 + tb * 512 + 512],
                                     start=(f == 0), stop=(f == FC - 1)),
                                 inc="te" if f == FC - 1 else None)
                        mm_dn = c.cnt["te"]
                        if FFN_DOWN >= 3:
                            c.wait("ve", "te", mm_dn)
                            ac = dd * 1024 + tb * 512
                            c.op("ve", lambda E, ac=ac, bk=bk: E.tensor_add(
                                A2[:, ac:ac + 512], A2[:, ac:ac + 512],
                                B[bk][:, 0:512]), inc="ve")
                    dslot_free[sl] = c.cnt["te"]
                    if dd + 2 < DC:
                        issue_down(dd + 2)
                if e == 0 and tap0 is not None:
                    c.wait("gp", "ve")
                    c.op("gp", lambda E: E.dma_start(
                        out=ap3(tap0, 0, [[TC, 128], [128 * TC, DC],
                                          [1, TC]]),
                        in_=ap3(A2, 0, [[16384, 128], [TC, DC], [1, TC]])),
                        inc="dg", inck=16)
                    c.wait("ve", "dg", c.cnt["dg"])

        def residual_out(base_dram, dst_dram):
            c.wait("gp", "dg")
            for half in range(2):
                c.wait("gp", "ve")
                c.wait("gp", "te")
                c.op("gp", lambda E, half=half: E.dma_start(
                    out=ap3(A1, half * 8192,
                            [[16384, 128], [TC, 8], [1, TC]]),
                    in_=ap3(base_dram, half * 8 * 128 * TC,
                            [[TC, 128], [128 * TC, 8], [1, TC]])),
                    inc="dg", inck=16)
                ld = c.cnt["dg"]
                c.wait("ve", "dg", ld)
                c.wait("ve", "te")
                for kk in range(8):
                    k = half * 8 + kk
                    ao = half * 8192 + kk * TC
                    c.op("ve", lambda E, k=k, ao=ao:
                         E.scalar_tensor_tensor(
                             A1[:, ao:ao + TC], A2[:, k * TC:k * TC + TC],
                             0.5, A1[:, ao:ao + TC], MUL, ADD), inc="ve")
            c.wait("gp", "ve")
            c.op("gp", lambda E: E.dma_start(
                out=ap3(dst_dram, 0, [[TC, 128], [128 * TC, DC], [1, TC]]),
                in_=ap3(A1, 0, [[16384, 128], [TC, DC], [1, TC]])),
                inc="dg", inck=16)

        def emit_av(kt, se_exp, te_av):
            p = kt % 2
            c.wait("te", "se", se_exp[kt])
            st, sp = (kt == 0), (kt == 31)
            e1 = EOF + p * 512
            e2 = EOF + (2 + p) * 512
            c.op("te", lambda E, kt=kt, e1=e1, st=st, sp=sp: E.matmul(
                B[4][:, 0:512], A4[:, kt * 128:kt * 128 + 128],
                A2[:, e1:e1 + 512], start=st, stop=sp))
            c.op("te", lambda E, kt=kt, e2=e2, st=st, sp=sp: E.matmul(
                B[5][:, 0:512], A4[:, kt * 128:kt * 128 + 128],
                A2[:, e2:e2 + 512], start=st, stop=sp))
            c.op("te", lambda E, e1=e1, st=st, sp=sp: E.matmul(
                B[6][:1, 0:512], ONESC, A2[:, e1:e1 + 512],
                start=st, stop=sp))
            c.op("te", lambda E, e2=e2, st=st, sp=sp: E.matmul(
                B[7][:1, 0:512], ONESC, A2[:, e2:e2 + 512],
                start=st, stop=sp), inc="te")
            te_av[kt] = c.cnt["te"]

        def _schedule():
            if PL < 1:
                raise _PhaseStop()
            for tb in range(2):
                ln_block(lambda k, tb=tb: A2[:, k * TC + tb * 512:
                                             k * TC + tb * 512 + 512],
                         lambda k, tb=tb: A1[:, k * TC + tb * 512:
                                             k * TC + tb * 512 + 512], 512)
            if PL < 2:
                raise _PhaseStop()
            cmb_dg = router(lambda k, tb: A1[:, k * TC + tb * 512:
                                             k * TC + tb * 512 + 512])
            if DEBUG_TAPS:
                c.wait("gp", "ve")
                c.wait("gp", "te")
                c.op("gp", lambda E: E.dma_start(
                    out=ap3(dbg["dbg_xa"], 0,
                            [[TC, 128], [128 * TC, DC], [1, TC]]),
                    in_=ap3(A1, 0, [[16384, 128], [TC, DC], [1, TC]])),
                    inc="dg", inck=16)
                tap_xa_dg = c.cnt["dg"]
                c.wait("te", "dg", tap_xa_dg)
            if PL < 3:
                raise _PhaseStop()
            ffn_experts(lambda k, tb: A1[:, k * TC + tb * 512:
                                         k * TC + tb * 512 + 512], cmb_dg,
                        tap0=dbg.get("dbg_acc0"))
            if DEBUG_TAPS:
                c.wait("gp", "ve")
                c.op("gp", lambda E: E.dma_start(out=dbg["dbg_cmb"].ap(),
                                                 in_=cmbD.ap()),
                     inc="dg", inck=16)
                c.op("gp", lambda E: E.dma_start(
                    out=ap3(dbg["dbg_acc"], 0,
                            [[TC, 128], [128 * TC, DC], [1, TC]]),
                    in_=ap3(A2, 0, [[16384, 128], [TC, DC], [1, TC]])),
                    inc="dg", inck=16)
                c.op("gp", lambda E: E.dma_start(
                    out=ap3(dbg["dbg_xln1"], 0,
                            [[TC, 128], [128 * TC, DC], [1, TC]]),
                    in_=ap3(A1, 0, [[16384, 128], [TC, DC], [1, TC]])),
                    inc="dg", inck=16)
                c.op("gp", lambda E: E.dma_start(
                    out=ap3(dbg["dbg_h"], 0, [[6144, 128], [1, 6144]]),
                    in_=ap3(A3, 0, [[8192, 128], [1, 6144]])),
                    inc="dg", inck=16)
            if PL < 4:
                raise _PhaseStop()
            residual_out(din["xT"], y1b)
            y1_st = c.cnt["dg"]

            if PL < 5:
                raise _PhaseStop()
            c.wait("gp", "dg", y1_st)
            c.op("gp", lambda E: E.collective_compute(
                "AllGather", mybir.AluOpType.bypass,
                replica_groups=[[0, 1, 2, 3], [4, 5, 6, 7]],
                ins=[y1b.ap().opt()], outs=[xg.ap().opt()]), inc="cc")

            if PL < 6:
                raise _PhaseStop()
            c.wait("sy", "cc")
            st_done = [0, 0]
            for blk in range(8):
                r, hf = blk // 2, blk % 2
                sl = blk % 2
                c.wait("sy", "dg", st_done[sl])
                c.wait("sy", "ve")
                c.wait("sy", "te")
                c.op("sy", lambda E, r=r, hf=hf, sl=sl: E.dma_start(
                    out=ap3(A1, sl * 8192, [[16384, 128], [512, DC], [1, 512]]),
                    in_=ap3(xg, r * D * TC + hf * 512,
                            [[TC, 128], [128 * TC, DC], [1, 512]])),
                    inc="dw", inck=16)
                ld = c.cnt["dw"]
                c.wait("ve", "dw", ld)
                ln_block(lambda k, sl=sl: A1[:, sl * 8192 + k * 512:
                                             sl * 8192 + k * 512 + 512],
                         lambda k, sl=sl: A1[:, sl * 8192 + k * 512:
                                             sl * 8192 + k * 512 + 512], 512)
                c.wait("gp", "ve")
                c.op("gp", lambda E, r=r, hf=hf, sl=sl: E.dma_start(
                    out=ap3(xlnD, r * D * TC + hf * 512,
                            [[TC, 128], [128 * TC, DC], [1, 512]]),
                    in_=ap3(A1, sl * 8192, [[16384, 128], [512, DC], [1, 512]])),
                    inc="dg", inck=16)
                st_done[sl] = c.cnt["dg"]
            ln2_st = c.cnt["dg"]

            if PL < 7:
                raise _PhaseStop()
            for hh in range(min(4, PL - 6)):
                c.wait("sy", "te")
                c.wait("sy", "ve")
                c.wait("sy", "se")
                if hh == 0:
                    c.wait("sy", "dg", ln2_st)
                    c.wait("ve", "dg", ln2_st)
                wdsts = (ap3(A4, 4096, [[6144, 128], [128, DC], [1, 128]]),
                         ap3(A3, 4096, [[8192, 128], [128, DC], [1, 128]]),
                         ap3(A3, 6144, [[8192, 128], [128, DC], [1, 128]]))
                for nm, dst in zip(("wq4", "wk4", "wv4"), wdsts):
                    c.op("sy", lambda E, nm=nm, dst=dst, hh=hh: E.dma_start(
                        out=dst,
                        in_=ap3(din[nm], hh * 128,
                                [[512, 128], [128 * 512, DC], [1, 128]])),
                        inc="dw", inck=16)
                w_ld = c.cnt["dw"]

                tile_free = [0, 0]
                for tbt in range(32):
                    r, pos = tbt // 8, tbt % 8
                    sl = tbt % 2
                    c.wait("sy", "te", tile_free[sl])
                    c.op("sy", lambda E, r=r, pos=pos, sl=sl: E.dma_start(
                        out=ap3(A3, sl * 2048,
                                [[8192, 128], [128, DC], [1, 128]]),
                        in_=ap3(xlnD, r * D * TC + pos * 128,
                                [[TC, 128], [128 * TC, DC], [1, 128]])),
                        inc="dw", inck=16)
                    ld = c.cnt["dw"]
                    c.wait("te", "dw", max(ld, w_ld))
                    c.wait("te", "ve")
                    wsrc = ((A4, 4096), (A3, 4096), (A3, 6144))
                    for pi in range(3):
                        wh, wo_ = wsrc[pi]
                        for k in range(DC):
                            c.op("te", lambda E, pi=pi, k=k, sl=sl, wh=wh,
                                 wo_=wo_: E.matmul(
                                     B[pi][:, 0:128],
                                     wh[:, wo_ + k * 128:wo_ + k * 128 + 128],
                                     A3[:, sl * 2048 + k * 128:
                                        sl * 2048 + k * 128 + 128],
                                     start=(k == 0), stop=(k == DC - 1)),
                                 inc="te" if k == DC - 1 else None)
                    qkv_done = c.cnt["te"]
                    tile_free[sl] = qkv_done
                    c.wait("ve", "te", qkv_done)
                    c.op("ve", lambda E, tbt=tbt: E.tensor_copy(
                        A2[:, QOF + tbt * 128:QOF + tbt * 128 + 128],
                        B[0][:, 0:128]))
                    c.op("ve", lambda E, tbt=tbt: E.tensor_copy(
                        A2[:, KOF + tbt * 128:KOF + tbt * 128 + 128],
                        B[1][:, 0:128]))
                    c.op("ve", lambda E: E.tensor_copy(
                        A2[:, VST:VST + 128], B[2][:, 0:128]), inc="ve")
                    vs_done = c.cnt["ve"]
                    c.wait("te", "ve", vs_done)
                    c.op("te", lambda E: E.matmul(
                        B[3][:, 0:128], A2[:, VST:VST + 128], IDT,
                        is_transpose=True, start=True, stop=True), inc="te")
                    c.wait("ve", "te")
                    c.op("ve", lambda E, tbt=tbt: E.tensor_copy(
                        A4[:, tbt * 128:tbt * 128 + 128], B[3][:, 0:128]),
                        inc="ve")

                def bound(off, qn_off):
                    c.wait("se", "ve")
                    c.wait("se", "te")
                    c.op("se", lambda E, off=off: E.activation(
                        A3[:, 4096:8192], A2[:, off:off + 4096], AF.Square),
                        inc="se")
                    sq = c.cnt["se"]
                    c.wait("te", "se", sq)
                    for blk in range(8):
                        c.wait("te", "ve")
                        c.op("te", lambda E, blk=blk: E.matmul(
                            B[5][:1, 0:512], A6[:, O_O64A:O_O64A + 1],
                            A3[:, 4096 + blk * 512:4096 + blk * 512 + 512],
                            start=True, stop=True))
                        c.op("te", lambda E, blk=blk: E.matmul(
                            B[6][:1, 0:512], A6[:, O_O64B:O_O64B + 1],
                            A3[:, 4096 + blk * 512:4096 + blk * 512 + 512],
                            start=True, stop=True), inc="te")
                        mmv = c.cnt["te"]
                        c.wait("ve", "te", mmv)
                        for hi, bk in ((0, 5), (1, 6)):
                            qo = qn_off + hi
                            if blk == 0:
                                c.op("ve", lambda E, bk=bk, qo=qo: E.reduce_max(
                                    A6[:1, qo:qo + 1], B[bk][:1, 0:512],
                                    axis=AX))
                            else:
                                c.op("ve", lambda E, bk=bk: E.reduce_max(
                                    A6[:1, O_SML:O_SML + 1], B[bk][:1, 0:512],
                                    axis=AX))
                                c.op("ve", lambda E, qo=qo: E.tensor_max(
                                    A6[:1, qo:qo + 1], A6[:1, qo:qo + 1],
                                    A6[:1, O_SML:O_SML + 1]))
                        c.op("ve", lambda E: E.tensor_copy(
                            A6[:1, O_SML + 1:O_SML + 2],
                            A6[:1, O_SML:O_SML + 1]), inc="ve")

                bound(QOF, O_QN)
                bound(KOF, O_QN + 2)
                c.wait("se", "ve")
                c.op("se", lambda E: E.activation(
                    A6[:1, O_QN:O_QN + 4], A6[:1, O_QN:O_QN + 4], AF.Sqrt),
                    inc="se")
                c.wait("ve", "se")
                c.op("ve", lambda E: E.tensor_mul(
                    A6[:1, O_QN + 4:O_QN + 6], A6[:1, O_QN:O_QN + 2],
                    A6[:1, O_QN + 2:O_QN + 4]))
                c.op("ve", lambda E: E.tensor_scalar(
                    A6[:1, O_QN + 4:O_QN + 6], A6[:1, O_QN + 4:O_QN + 6],
                    -SCALE, None, MUL), inc="ve")
                c.wait("te", "ve")
                c.op("te", lambda E: E.matmul(
                    B[4][:, 0:1], ONR, A6[:1, O_QN + 4:O_QN + 5],
                    start=True, stop=True))
                c.op("te", lambda E: E.matmul(
                    B[5][:, 0:1], ONR, A6[:1, O_QN + 5:O_QN + 6],
                    start=True, stop=True), inc="te")
                c.wait("ve", "te")
                c.op("ve", lambda E: E.tensor_copy(A6[:, O_NC1:O_NC1 + 1],
                                                   B[4][:, 0:1]))
                c.op("ve", lambda E: E.tensor_copy(A6[:, O_NC2:O_NC2 + 1],
                                                   B[5][:, 0:1]), inc="ve")

                for qb in range(8):
                    se_exp = {}
                    te_av = {}
                    for kt in range(32):
                        p = kt % 2
                        if kt == 0:
                            c.wait("te", "ve")
                        if kt >= 2:
                            c.wait("te", "se", se_exp[kt - 2])
                        for half in range(2):
                            bnk = (0 if half == 0 else 2) + p
                            po = half * 64
                            c.op("te", lambda E, kt=kt, po=po, bnk=bnk, qb=qb:
                                 E.matmul(
                                     B[bnk][:, 0:512],
                                     A2[po:po + 64,
                                        KOF + kt * 128:KOF + kt * 128 + 128],
                                     A2[po:po + 64,
                                        QOF + qb * 512:QOF + qb * 512 + 512],
                                     start=True, stop=True),
                                 inc="te" if half == 1 else None)
                        te_scr = c.cnt["te"]
                        if kt >= 1:
                            emit_av(kt - 1, se_exp, te_av)
                        c.wait("se", "te", te_scr)
                        if kt >= 2:
                            c.wait("se", "te", te_av.get(kt - 2, 0))
                        for half in range(2):
                            bnk = (0 if half == 0 else 2) + p
                            nco = O_NC1 if half == 0 else O_NC2
                            eo = EOF + (half * 2 + p) * 512
                            c.op("se", lambda E, bnk=bnk, nco=nco, eo=eo:
                                 E.activation(
                                     A2[:, eo:eo + 512], B[bnk][:, 0:512],
                                     AF.Exp, bias=A6[:, nco:nco + 1],
                                     scale=SCALE),
                                 inc="se" if half == 1 else None)
                        se_exp[kt] = c.cnt["se"]
                    emit_av(31, se_exp, te_av)
                    c.wait("ve", "te", te_av[31])
                    c.op("ve", lambda E: E.reciprocal(A6[:1, O_ROW:O_ROW + 512],
                                                      B[6][:1, 0:512]))
                    c.op("ve", lambda E: E.reciprocal(
                        A6[:1, O_ROW2:O_ROW2 + 512], B[7][:1, 0:512]))
                    c.op("ve", lambda E: E.tensor_scalar(
                        A6[:1, O_ROW2:O_ROW2 + 512],
                        A6[:1, O_ROW2:O_ROW2 + 512],
                        A6[:1, O_LAM:O_LAM + 1], None, MUL), inc="ve")
                    c.wait("te", "ve")
                    c.op("te", lambda E: E.matmul(
                        B[0][:, 0:512], ONR, A6[:1, O_ROW:O_ROW + 512],
                        start=True, stop=True))
                    c.op("te", lambda E: E.matmul(
                        B[1][:, 0:512], ONR, A6[:1, O_ROW2:O_ROW2 + 512],
                        start=True, stop=True), inc="te")
                    c.wait("ve", "te")
                    c.op("ve", lambda E: E.tensor_copy(A3[:, 4096:4608],
                                                       B[4][:, 0:512]))
                    c.op("ve", lambda E: E.tensor_copy(A3[:, 4608:5120],
                                                       B[5][:, 0:512]))
                    c.op("ve", lambda E: E.tensor_mul(A3[:, 4096:4608],
                                                      A3[:, 4096:4608],
                                                      B[0][:, 0:512]))
                    c.op("ve", lambda E: E.tensor_mul(A3[:, 4608:5120],
                                                      A3[:, 4608:5120],
                                                      B[1][:, 0:512]))
                    c.op("ve", lambda E, qb=qb: E.tensor_sub(
                        A2[:, OOF + qb * 512:OOF + qb * 512 + 512],
                        A3[:, 4096:4608], A3[:, 4608:5120]), inc="ve")

                c.wait("se", "ve")
                c.wait("se", "te")
                c.op("se", lambda E: E.activation(A3[:, 4096:8192],
                                                  A2[:, OOF:OOF + 4096],
                                                  AF.Square), inc="se")
                sq_rms = c.cnt["se"]
                c.wait("te", "se", sq_rms)
                for blk in range(8):
                    c.wait("te", "ve")
                    c.op("te", lambda E, blk=blk: E.matmul(
                        B[6][:1, 0:512], ONESC,
                        A3[:, 4096 + blk * 512:4096 + blk * 512 + 512],
                        start=True, stop=True), inc="te")
                    mm_s = c.cnt["te"]
                    c.wait("ve", "te", mm_s)
                    c.op("ve", lambda E: E.tensor_scalar(
                        A6[:1, O_ROW:O_ROW + 512], B[6][:1, 0:512], 1.0 / 128,
                        None, MUL), inc="ve")
                    r_ve = c.cnt["ve"]
                    c.wait("se", "ve", r_ve)
                    c.op("se", lambda E: E.activation(
                        A6[:1, O_ROW:O_ROW + 512], A6[:1, O_ROW:O_ROW + 512],
                        AF.Sqrt, bias=A6[:1, O_EPS2:O_EPS2 + 1]), inc="se")
                    c.wait("ve", "se")
                    c.op("ve", lambda E: E.reciprocal(
                        A6[:1, O_ROW:O_ROW + 512], A6[:1, O_ROW:O_ROW + 512]))
                    c.op("ve", lambda E: E.tensor_scalar(
                        A6[:1, O_ROW:O_ROW + 512], A6[:1, O_ROW:O_ROW + 512],
                        1.0 - LAMBDA_INIT, None, MUL), inc="ve")
                    c.wait("te", "ve")
                    c.op("te", lambda E: E.matmul(
                        B[4][:, 0:512], ONR, A6[:1, O_ROW:O_ROW + 512],
                        start=True, stop=True), inc="te")
                    c.wait("ve", "te")
                    c.op("ve", lambda E, blk=blk: E.tensor_mul(
                        A2[:, OOF + blk * 512:OOF + blk * 512 + 512],
                        A2[:, OOF + blk * 512:OOF + blk * 512 + 512],
                        B[4][:, 0:512]), inc="ve")

                for lp in range(32):
                    s_hi, lo = lp // 16, lp % 16
                    cpr = 2 * hh + s_hi
                    base = OOF + s_hi * 64 + 4 * lo
                    c.op("ve", lambda E, base=base: E.tensor_copy(
                        A2[:, VST:VST + 128],
                        ap3(A2, base, [[16384, 128], [1, 4], [1024, 4],
                                       [128, 8]])), inc="ve")
                    g_done = c.cnt["ve"]
                    c.wait("te", "ve", g_done)
                    bnk = 2 + lp % 2
                    c.op("te", lambda E, bnk=bnk: E.matmul(
                        B[bnk][:, 0:128], A2[:, VST:VST + 128], IDT,
                        is_transpose=True, start=True, stop=True), inc="te")
                    c.wait("ve", "te")
                    c.op("ve", lambda E, lo=lo, cpr=cpr, bnk=bnk: E.tensor_copy(
                        A1[:, lo * 1024 + cpr * 128:
                           lo * 1024 + cpr * 128 + 128], B[bnk][:, 0:128]),
                        inc="ve")

            if PL < 11:
                raise _PhaseStop()
            c.wait("sy", "te")
            c.wait("sy", "ve")
            c.wait("sy", "se")
            wo_free = [0, 0]
            res_free = [0, 0]
            for m in range(DC):
                wsl = m % 2
                wdst = (ap3(A4, 4096, [[6144, 128], [128, DC], [1, 128]])
                        if wsl == 0 else
                        ap3(A3, 4096, [[8192, 128], [128, DC], [1, 128]]))
                c.wait("sy", "te", wo_free[wsl])
                c.op("sy", lambda E, m=m, wdst=wdst: E.dma_start(
                    out=wdst,
                    in_=ap3(din["woT"], m * 128,
                            [[D, 128], [128 * D, DC], [1, 128]])),
                    inc="dw", inck=16)
                w_ld = c.cnt["dw"]
                rsl = m % 2
                c.wait("sy", "ve", res_free[rsl])
                c.op("sy", lambda E, m=m, rsl=rsl: E.dma_start(
                    out=ap3(A3, 6144 + rsl * 1024, [[8192, 128], [1, TC]]),
                    in_=ap3(y1b, m * 128 * TC, [[TC, 128], [1, TC]])),
                    inc="dw", inck=16)
                r_ld = c.cnt["dw"]
                for jb in range(2):
                    bnk = jb
                    c.wait("te", "dw", w_ld)
                    c.wait("te", "ve")
                    for lo in range(DC):
                        if wsl == 0:
                            wap = A4[:, 4096 + lo * 128:4096 + lo * 128 + 128]
                        else:
                            wap = A3[:, 4096 + lo * 128:4096 + lo * 128 + 128]
                        c.op("te", lambda E, lo=lo, jb=jb, bnk=bnk, wap=wap:
                             E.matmul(
                                 B[bnk][:, 0:512], wap,
                                 A1[:, lo * 1024 + jb * 512:
                                    lo * 1024 + jb * 512 + 512],
                                 start=(lo == 0), stop=(lo == DC - 1)),
                             inc="te" if lo == DC - 1 else None)
                    mm_done = c.cnt["te"]
                    c.wait("ve", "te", mm_done)
                    c.wait("ve", f"r{rsl}")
                    c.op("ve", lambda E, m=m, jb=jb, bnk=bnk, rsl=rsl:
                         E.scalar_tensor_tensor(
                             A2[:, m * TC + jb * 512:m * TC + jb * 512 + 512],
                             B[bnk][:, 0:512], 1.0,
                             A3[:, 6144 + rsl * 1024 + jb * 512:
                                6144 + rsl * 1024 + jb * 512 + 512],
                             MUL, ADD), inc="ve")
                wo_free[wsl] = c.cnt["te"]
                res_free[rsl] = c.cnt["ve"]
                c.wait("gp", "ve")
                c.op("gp", lambda E, m=m: E.dma_start(
                    out=ap3(x2b, m * 128 * TC, [[TC, 128], [1, TC]]),
                    in_=A2[:, m * TC:m * TC + TC]), inc="dg", inck=16)

            if PL < 12:
                raise _PhaseStop()
            for tb in range(2):
                ln_block(lambda k, tb=tb: A2[:, k * TC + tb * 512:
                                             k * TC + tb * 512 + 512],
                         lambda k, tb=tb: A1[:, k * TC + tb * 512:
                                             k * TC + tb * 512 + 512], 512)
            if PL < 13:
                raise _PhaseStop()
            cmb_dg2 = router(lambda k, tb: A1[:, k * TC + tb * 512:
                                              k * TC + tb * 512 + 512])
            if PL < 14:
                raise _PhaseStop()
            ffn_experts(lambda k, tb: A1[:, k * TC + tb * 512:
                                         k * TC + tb * 512 + 512], cmb_dg2)
            if PL < 15:
                raise _PhaseStop()
            residual_out(x2b, outT)

        try:
            _schedule()
        except _PhaseStop:
            c.wait("gp", "ve")
            c.wait("gp", "te")
            c.wait("gp", "se")
            c.wait("gp", "dw")
            c.op("gp", lambda E: E.dma_start(
                out=ap3(outT, 0, [[TC, 128], [128 * TC, DC], [1, TC]]),
                in_=ap3(A1, 0, [[16384, 128], [TC, DC], [1, TC]])),
                inc="dg", inck=16)

        _phase_stopped = False
        if DEBUG_TAPS:
            c.wait("gp", "dg")
            c.wait("gp", "cc")
            c.op("gp", lambda E: E.dma_start(out=dbg["dbg_y1"].ap(),
                                             in_=y1b.ap()),
                 inc="dg", inck=16)
            c.op("gp", lambda E: E.dma_start(out=dbg["dbg_xg"].ap(),
                                             in_=xg.ap()),
                 inc="dg", inck=16)
            c.op("gp", lambda E: E.dma_start(out=dbg["dbg_xln"].ap(),
                                             in_=xlnD.ap()),
                 inc="dg", inck=16)
            c.op("gp", lambda E: E.dma_start(out=dbg["dbg_x2"].ap(),
                                             in_=x2b.ap()),
                 inc="dg", inck=16)
        c.wait("gp", "dg")

        with nc.Block() as blk:
            @blk.tensor
            def _(E):
                for f in c.ops["te"]:
                    f(E)

            @blk.vector
            def _(E):
                for f in c.ops["ve"]:
                    f(E)

            @blk.scalar
            def _(E):
                for f in c.ops["se"]:
                    f(E)

            @blk.sync
            def _(E):
                for f in c.ops["sy"]:
                    f(E)

            @blk.gpsimd
            def _(E):
                for f in c.ops["gp"]:
                    f(E)

    return nc


def _mk_host_constants():
    ones64a = np.zeros((128, 1), np.float32)
    ones64a[:64] = 1.0
    ones64b = np.zeros((128, 1), np.float32)
    ones64b[64:] = 1.0
    return {
        "id128": np.eye(128, dtype=np.float32),
        "ones128": np.ones((128, 1), np.float32),
        "onesr": np.ones((1, 128), np.float32),
        "ones64a": ones64a,
        "ones64b": ones64b,
        "epsln": np.full((128, 1), LN_EPS, np.float32),
        "epsrms": np.full((128, 1), RMS_EPS, np.float32),
    }


def _token_perm(g):
    j = np.arange(TC)
    e = j % 128
    cpr = j // 128
    hprime = cpr // 2
    s_hi = cpr % 2
    return 32 * e + 8 * g + 2 * hprime + s_hi


def kernel(L0_attn_x, L0_attn_wq, L0_attn_wk, L0_attn_wv, L0_attn_wo,
           L0_attn_lambda_q1, L0_attn_lambda_q2, L0_attn_lambda_k1,
           L0_attn_lambda_k2, L0_ffn_router_weight, L0_ffn_gate_up_proj,
           L0_ffn_down_proj):
    x = np.asarray(L0_attn_x, np.float32)
    wq = np.asarray(L0_attn_wq, np.float32)
    wk = np.asarray(L0_attn_wk, np.float32)
    wv = np.asarray(L0_attn_wv, np.float32)
    wo = np.asarray(L0_attn_wo, np.float32)
    rw = np.asarray(L0_ffn_router_weight, np.float32)
    gup = np.ascontiguousarray(np.asarray(L0_ffn_gate_up_proj, np.float32))
    dwn = np.ascontiguousarray(np.asarray(L0_ffn_down_proj, np.float32))
    consts = _mk_host_constants()

    wqT = np.ascontiguousarray(wq.T)
    wkT = np.ascontiguousarray(wk.T)
    wvT = np.ascontiguousarray(wv.T)
    woT = np.ascontiguousarray(wo.T)
    rwT = np.ascontiguousarray(rw.T)

    def chunks(a, ncol):
        return np.ascontiguousarray(a.reshape(DC, 128, ncol))

    in_maps = []
    for core in range(8):
        bc, g = core // 4, core % 4
        tp = _token_perm(g)
        xTc = np.ascontiguousarray(x[bc].T[:, tp])
        m = {
            "xT": chunks(xTc, TC),
            "rwT": chunks(rwT, NE),
            "gu": gup,
            "dwn": dwn,
            "wq4": chunks(np.ascontiguousarray(
                wqT[:, 512 * g:512 * g + 512]), 512),
            "wk4": chunks(np.ascontiguousarray(
                wkT[:, 512 * g:512 * g + 512]), 512),
            "wv4": chunks(np.ascontiguousarray(
                wvT[:, 512 * g:512 * g + 512]), 512),
            "woT": chunks(woT, D),
            "lq1": np.asarray(L0_attn_lambda_q1, np.float32).reshape(128, 1),
            "lq2": np.asarray(L0_attn_lambda_q2, np.float32).reshape(128, 1),
            "lk1": np.asarray(L0_attn_lambda_k1, np.float32).reshape(128, 1),
            "lk2": np.asarray(L0_attn_lambda_k2, np.float32).reshape(128, 1),
        }
        m.update(consts)
        in_maps.append(m)

    nc = build_program()
    res = run_bass_kernel_spmd(nc, in_maps, core_ids=list(range(8)))

    out = np.empty((2, S, D), np.float32)
    for core in range(8):
        bc, g = core // 4, core % 4
        tp = _token_perm(g)
        oT = np.asarray(res.results[core]["outT"]).reshape(D, TC)
        out[bc, tp, :] = oT.T
    return out


# version 06 (final)
# speedup vs baseline: 1.0910x; 1.0910x over previous
import contextlib
import numpy as np
import concourse.bass as bass
import concourse.mybir as mybir
from concourse.bass_utils import run_bass_kernel_spmd
from concourse.alu_op_type import AluOpType

F32 = mybir.dt.float32
BF16 = mybir.dt.bfloat16
AF = mybir.ActivationFunctionType
AX = mybir.AxisListType.X
ADD, SUB, MUL = AluOpType.add, AluOpType.subtract, AluOpType.mult

D = 2048
DC = 16
TC = 1024
S = 4096
NE = 16
FI = 768
FC = 6
SCALE = float(1.0 / np.sqrt(np.float32(128.0)))
LN_EPS = 1e-5
RMS_EPS = 1.1920928955078125e-07
LAMBDA_INIT = 0.8

ENGS = ("te", "ve", "se", "sy", "gp")
DEBUG_TAPS = False
PHASE_LIMIT = 99
FFN_NE = 16
FFN_GATE = True
FFN_DOWN = 3


class Sched:

    def __init__(self, nc):
        self.nc = nc
        self.ops = {e: [] for e in ENGS}
        self.sem = {}
        self.cnt = {}

    def mksem(self, stack, *names):
        for n in names:
            self.sem[n] = stack.enter_context(self.nc.semaphore(n))
            self.cnt[n] = 0

    def wait(self, eng, sname, val=None):
        v = self.cnt[sname] if val is None else val
        if v <= 0:
            return
        s = self.sem[sname]
        self.ops[eng].append(lambda E: E.wait_ge(s, v))

    def op(self, eng, fn, inc=None, inck=1, drain=False):
        if inc is None:
            self.ops[eng].append(fn)
        else:
            s = self.sem[inc]
            self.ops[eng].append(lambda E: fn(E).then_inc(s, inck))
            self.cnt[inc] += inck
        if drain:
            self.ops[eng].append(lambda E: E.drain())
        return self.cnt[inc] if inc is not None else None


class _PhaseStop(Exception):
    pass


def ap3(h, off, dims):
    t = getattr(h, "tensor", h)
    return bass.AP(t, off, dims)


def build_program(phase_limit=None):
    PL = PHASE_LIMIT if phase_limit is None else phase_limit
    nc = bass.Bass("TRN2", target_bir_lowering=False,
                   detect_race_conditions=False)

    din = {}
    for name, shape in [
            ("xT", [DC, 128, TC]), ("rwT", [DC, 128, NE]),
            ("gu", [NE, D, 2 * FI]), ("dwn", [NE, FI, D]),
            ("wq4", [DC, 128, 512]), ("wk4", [DC, 128, 512]),
            ("wv4", [DC, 128, 512]), ("woT", [DC, 128, D]),
            ("lq1", [128, 1]), ("lq2", [128, 1]), ("lk1", [128, 1]),
            ("lk2", [128, 1]), ("id128", [128, 128]), ("ones128", [128, 1]),
            ("onesr", [1, 128]), ("ones64a", [128, 1]), ("ones64b", [128, 1]),
            ("epsln", [128, 1]), ("epsrms", [128, 1])]:
        din[name] = nc.dram_tensor(name, shape, F32, kind="ExternalInput")
    outT = nc.dram_tensor("outT", [DC, 128, TC], F32, kind="ExternalOutput")
    dbg = {}
    if DEBUG_TAPS:
        for nm, shape in (("dbg_y1", [D, TC]), ("dbg_xg", [4, D, TC]),
                          ("dbg_xln", [4, D, TC]), ("dbg_x2", [D, TC]),
                          ("dbg_cmb", [NE, TC]), ("dbg_acc", [D, TC]),
                          ("dbg_xln1", [D, TC]), ("dbg_h", [128, 6144]),
                          ("dbg_xa", [D, TC]), ("dbg_acc0", [D, TC])):
            dbg[nm] = nc.dram_tensor(nm, shape, F32, kind="ExternalOutput")

    y1b = nc.dram_tensor("y1b", [D, TC], F32)
    xg = nc.dram_tensor("xg", [4, D, TC], F32)
    xlnD = nc.dram_tensor("xlnD", [4, D, TC], F32)
    x2b = nc.dram_tensor("x2b", [D, TC], F32)
    cmbD = nc.dram_tensor("cmbD", [NE, TC], F32)

    stack = contextlib.ExitStack()
    with stack:
        sbuf = lambda name, w: stack.enter_context(
            nc.sbuf_tensor(name, [128, w], F32))
        A1 = sbuf("A1", 16384)
        A2 = sbuf("A2", 16384)
        A3 = sbuf("A3", 8192)
        A4 = sbuf("A4", 6144)
        A5 = sbuf("A5", 1536)
        A6 = sbuf("A6", 2688)
        B = [stack.enter_context(nc.psum_tensor(f"B{i}", [128, 512], F32))
             for i in range(8)]

        c = Sched(nc)
        c.mksem(stack, "te", "ve", "se", "dw", "dg", "cc",
                "g0", "g1", "g2", "d0", "d1", "cr",
                "q0", "q1", "qw", "w0", "w1", "r0", "r1",
                "l0", "l1", "s0", "s1")

        O_ID = 0
        O_ONES = 128
        O_O64A = 129
        O_O64B = 130
        O_ONR = 131
        O_LAM = 259
        O_NC1 = 260
        O_NC2 = 261
        O_LQ = 262
        O_QN = 266
        O_EPS1 = 336
        O_EPS2 = 337
        O_SML = 274
        O_ROW = 338
        O_ROW2 = 850
        O_CROW = 1362
        O_RW = 2386

        ONESC = A6[:, O_ONES:O_ONES + 1]
        ONR = A6[:1, O_ONR:O_ONR + 128]
        IDT = A6[:, O_ID:O_ID + 128]

        SQ0, SQ1 = 0, 512
        SILU0, SILU1 = 6144, 6656
        QOF, KOF, OOF, EOF, VST = 0, 4096, 8192, 12288, 14336

        def bfv(h, col, w):
            return h[:, col:col + w].bitcast(BF16)

        def xh_t(k, tb):
            return bfv(A1, k * 512 + tb * 256, 256)

        def xl_t(k, tb):
            return bfv(A1, 8192 + k * 512 + tb * 256, 256)

        def rwh_t(k):
            return bfv(A6, O_RW + k * 8, 8)

        def rwl_t(k):
            return bfv(A6, O_RW + 128 + k * 8, 8)

        LN_STAGE = A3[:, 7168:7680]

        c.op("sy", lambda E: E.dma_start(out=IDT, in_=din["id128"].ap()),
             inc="dw", inck=16)
        c.op("sy", lambda E: E.dma_start(out=ONESC, in_=din["ones128"].ap()),
             inc="dw", inck=16)
        c.op("sy", lambda E: E.dma_start(out=A6[:, O_O64A:O_O64A + 1],
                                         in_=din["ones64a"].ap()),
             inc="dw", inck=16)
        c.op("sy", lambda E: E.dma_start(out=A6[:, O_O64B:O_O64B + 1],
                                         in_=din["ones64b"].ap()),
             inc="dw", inck=16)
        c.op("sy", lambda E: E.dma_start(out=ONR, in_=din["onesr"].ap()),
             inc="dw", inck=16)
        for i, nm in enumerate(("lq1", "lq2", "lk1", "lk2")):
            c.op("sy", lambda E, i=i, nm=nm: E.dma_start(
                out=A6[:, O_LQ + i:O_LQ + i + 1], in_=din[nm].ap()),
                inc="dw", inck=16)
        c.op("sy", lambda E: E.dma_start(
            out=A6[:, O_EPS1:O_EPS1 + 1], in_=din["epsln"].ap()),
             inc="dw", inck=16)
        c.op("sy", lambda E: E.dma_start(
            out=A6[:, O_EPS2:O_EPS2 + 1], in_=din["epsrms"].ap()),
             inc="dw", inck=16)
        c.op("sy", lambda E: E.dma_start(
            out=ap3(A1, 0, [[16384, 128], [16, DC], [1, NE]]),
            in_=ap3(din["rwT"], 0, [[NE, 128], [128 * NE, DC], [1, NE]])),
             inc="dw", inck=16)
        c.op("sy", lambda E: E.dma_start(
            out=ap3(A2, 0, [[16384, 128], [TC, DC], [1, TC]]),
            in_=ap3(din["xT"], 0, [[TC, 128], [128 * TC, DC], [1, TC]])),
             inc="dw", inck=16)

        c.wait("te", "dw")
        c.op("te", lambda E: E.matmul(B[6][:1, 0:1], A6[:, O_LQ:O_LQ + 1],
                                      A6[:, O_LQ + 2:O_LQ + 3],
                                      start=True, stop=True))
        c.op("te", lambda E: E.matmul(B[7][:1, 0:1], A6[:, O_LQ + 1:O_LQ + 2],
                                      A6[:, O_LQ + 3:O_LQ + 4],
                                      start=True, stop=True), inc="te")
        c.wait("se", "te")
        c.op("se", lambda E: E.activation(A6[:1, O_SML:O_SML + 1],
                                          B[6][:1, 0:1], AF.Exp))
        c.op("se", lambda E: E.activation(A6[:1, O_SML + 1:O_SML + 2],
                                          B[7][:1, 0:1], AF.Exp), inc="se")
        c.wait("ve", "se")
        c.op("ve", lambda E: E.tensor_sub(A6[:1, O_LAM:O_LAM + 1],
                                          A6[:1, O_SML:O_SML + 1],
                                          A6[:1, O_SML + 1:O_SML + 2]),
             drain=True)
        c.op("ve", lambda E: E.tensor_scalar(A6[:1, O_LAM:O_LAM + 1],
                                             A6[:1, O_LAM:O_LAM + 1],
                                             LAMBDA_INIT, None, ADD),
             inc="ve", drain=True)
        c.wait("ve", "dw")
        c.op("ve", lambda E: E.tensor_copy(bfv(A6, O_RW, 128),
                                           A1[:, 0:256]), drain=True)
        c.op("ve", lambda E: E.tensor_sub(bfv(A6, O_RW + 128, 128),
                                          A1[:, 0:256],
                                          bfv(A6, O_RW, 128)),
             inc="ve", drain=True)

        def ln_block(src, dst, tw, split=None):
            c.wait("te", "ve")
            c.wait("te", "dw")
            c.wait("se", "ve")
            c.wait("se", "te")
            c.wait("se", "dw")
            for k in range(DC):
                c.op("te", lambda E, k=k: E.matmul(
                    B[5][:1, 0:tw], ONESC, src(k),
                    start=(k == 0), stop=(k == DC - 1)),
                    inc="te" if k == DC - 1 else None)
            te_sq = {}
            for k in range(DC):
                sl = SQ0 if k % 2 == 0 else SQ1
                if k >= 2:
                    c.wait("se", "te", te_sq[k - 2])
                c.op("se", lambda E, k=k, sl=sl: E.activation(
                    A3[:, sl:sl + tw], src(k), AF.Square), inc="se")
                se_k = c.cnt["se"]
                c.wait("te", "se", se_k)
                te_sq[k] = c.op("te", lambda E, k=k, sl=sl: E.matmul(
                    B[6][:1, 0:tw], ONESC, A3[:, sl:sl + tw],
                    start=(k == 0), stop=(k == DC - 1)), inc="te")
            c.wait("ve", "te")
            c.op("ve", lambda E: E.tensor_scalar(
                A6[:1, O_ROW:O_ROW + tw], B[5][:1, 0:tw], 1.0 / D, None,
                MUL), drain=True)
            c.op("ve", lambda E: E.tensor_scalar(
                A6[:1, O_ROW2:O_ROW2 + tw], B[6][:1, 0:tw], 1.0 / D, None,
                MUL), drain=True)
            c.op("ve", lambda E: E.tensor_mul(
                B[6][:1, 0:tw], A6[:1, O_ROW:O_ROW + tw],
                A6[:1, O_ROW:O_ROW + tw]), drain=True)
            c.op("ve", lambda E: E.tensor_sub(
                A6[:1, O_ROW2:O_ROW2 + tw], A6[:1, O_ROW2:O_ROW2 + tw],
                B[6][:1, 0:tw]), inc="ve", drain=True)
            c.wait("se", "ve")
            c.op("se", lambda E: E.activation(
                A6[:1, O_ROW2:O_ROW2 + tw], A6[:1, O_ROW2:O_ROW2 + tw],
                AF.Sqrt, bias=A6[:1, O_EPS1:O_EPS1 + 1]), inc="se")
            c.wait("ve", "se")
            c.op("ve", lambda E: E.reciprocal(
                A6[:1, O_ROW2:O_ROW2 + tw], A6[:1, O_ROW2:O_ROW2 + tw]),
                drain=True)
            c.op("ve", lambda E: E.tensor_mul(
                A6[:1, O_ROW:O_ROW + tw], A6[:1, O_ROW:O_ROW + tw],
                A6[:1, O_ROW2:O_ROW2 + tw]), inc="ve", drain=True)
            c.wait("te", "ve")
            c.op("te", lambda E: E.matmul(B[4][:, 0:tw], ONR,
                                          A6[:1, O_ROW2:O_ROW2 + tw],
                                          start=True, stop=True))
            c.op("te", lambda E: E.matmul(B[5][:, 0:tw], ONR,
                                          A6[:1, O_ROW:O_ROW + tw],
                                          start=True, stop=True), inc="te")
            c.wait("ve", "te")
            if split is None:
                for k in range(DC):
                    c.op("ve", lambda E, k=k: E.tensor_mul(dst(k), src(k),
                                                           B[4][:, 0:tw]))
                    c.op("ve", lambda E, k=k: E.tensor_sub(dst(k), dst(k),
                                                           B[5][:, 0:tw]),
                         inc="ve" if k == DC - 1 else None)
            else:
                stage, xh_fn, xl_fn = split
                for k in range(DC):
                    c.op("ve", lambda E, k=k: E.tensor_mul(stage, src(k),
                                                           B[4][:, 0:tw]))
                    c.op("ve", lambda E, k=k: E.tensor_sub(stage, stage,
                                                           B[5][:, 0:tw]),
                         drain=True)
                    c.op("ve", lambda E, k=k: E.tensor_copy(xh_fn(k), stage),
                         drain=True)
                    c.op("ve", lambda E, k=k: E.tensor_sub(xl_fn(k), stage,
                                                           xh_fn(k)),
                         inc="ve" if k == DC - 1 else None, drain=True)

        def router(xh, xl):
            def vop(fn, inc=None, inck=1, drain=True):
                return c.op("ve", fn, inc=inc, inck=inck, drain=True)
            LG, LGT = 0, 512
            PP, RR, SEL, KP = 576, 592, 608, 624
            MM_, NM, SM4, IV = 640, 641, 642, 643
            CMT, CST = 648, 664
            for tb in range(2):
                c.wait("te", "ve")
                c.wait("te", "dw")
                for k in range(DC):
                    for p, (wv, xv) in enumerate((
                            (rwh_t(k), xh(k, tb)), (rwh_t(k), xl(k, tb)),
                            (rwl_t(k), xh(k, tb)))):
                        c.op("te", lambda E, wv=wv, xv=xv, k=k, p=p:
                             E.matmul(
                                 B[6][:16, 0:512], wv, xv,
                                 start=(k == 0 and p == 0),
                                 stop=(k == DC - 1 and p == 2)),
                             inc="te" if (k == DC - 1 and p == 2) else None)
                c.wait("ve", "te")
                vop( lambda E: E.tensor_copy(A5[:16, LG:LG + 512],
                                                   B[6][:16, 0:512]),
                     inc="ve", drain=True)
                for j in range(4):
                    c.wait("te", "ve")
                    c.op("te", lambda E, j=j: E.matmul(
                        B[7][:, 0:16],
                        A5[:16, LG + j * 128:LG + j * 128 + 128],
                        IDT[:16, 0:16], is_transpose=True,
                        start=True, stop=True), inc="te")
                    c.wait("ve", "te")
                    vop( lambda E, j=j: E.tensor_copy(
                        A5[:, LGT + j * 16:LGT + j * 16 + 16], B[7][:, 0:16]),
                        inc="ve", drain=True)
                for j in range(4):
                    vop( lambda E, j=j: E.reduce_max(
                        A5[:, MM_:MM_ + 1],
                        A5[:, LGT + j * 16:LGT + j * 16 + 16], axis=AX))
                    vop( lambda E: E.tensor_scalar(
                        A5[:, NM:NM + 1], A5[:, MM_:MM_ + 1], -1.0, None,
                        MUL), inc="ve", drain=True)
                    c.wait("se", "ve")
                    c.op("se", lambda E, j=j: E.activation(
                        A5[:, PP:PP + 16],
                        A5[:, LGT + j * 16:LGT + j * 16 + 16], AF.Exp,
                        bias=A5[:, NM:NM + 1],
                        accum_out=A5[:, IV:IV + 1]), inc="se", drain=True)
                    c.wait("ve", "se")
                    vop( lambda E: E.reciprocal(A5[:, MM_:MM_ + 1],
                                                      A5[:, IV:IV + 1]))
                    vop( lambda E: E.tensor_scalar(
                        A5[:, PP:PP + 16], A5[:, PP:PP + 16],
                        A5[:, MM_:MM_ + 1], None, MUL))
                    vop( lambda E: E.tensor_copy(A5[:, RR:RR + 16],
                                                       A5[:, PP:PP + 16]))
                    for r in range(4):
                        vop( lambda E: E.reduce_max(
                            A5[:, MM_:MM_ + 1], A5[:, RR:RR + 16], axis=AX))
                        vop( lambda E: E.tensor_scalar(
                            A5[:, KP:KP + 16], A5[:, RR:RR + 16],
                            A5[:, MM_:MM_ + 1], None, AluOpType.is_ge))
                        if r == 0:
                            vop( lambda E: E.tensor_copy(
                                A5[:, SEL:SEL + 16], A5[:, KP:KP + 16]))
                            vop( lambda E: E.tensor_copy(
                                A5[:, SM4:SM4 + 1], A5[:, MM_:MM_ + 1]))
                        else:
                            vop( lambda E: E.tensor_add(
                                A5[:, SEL:SEL + 16], A5[:, SEL:SEL + 16],
                                A5[:, KP:KP + 16]))
                            vop( lambda E: E.tensor_add(
                                A5[:, SM4:SM4 + 1], A5[:, SM4:SM4 + 1],
                                A5[:, MM_:MM_ + 1]))
                        if r < 3:
                            vop( lambda E: E.tensor_scalar(
                                A5[:, KP:KP + 16], A5[:, KP:KP + 16],
                                -1.0, 1.0, MUL, ADD))
                            vop( lambda E: E.tensor_mul(
                                A5[:, RR:RR + 16], A5[:, RR:RR + 16],
                                A5[:, KP:KP + 16]))
                    vop( lambda E: E.reciprocal(A5[:, IV:IV + 1],
                                                      A5[:, SM4:SM4 + 1]))
                    vop( lambda E: E.tensor_mul(
                        A5[:, CMT:CMT + 16], A5[:, PP:PP + 16],
                        A5[:, SEL:SEL + 16]))
                    vop( lambda E: E.tensor_scalar(
                        A5[:, CMT:CMT + 16], A5[:, CMT:CMT + 16],
                        A5[:, IV:IV + 1], None, MUL), inc="ve", drain=True)
                    c.wait("te", "ve")
                    c.op("te", lambda E: E.matmul(
                        B[7][:16, 0:128], A5[:, CMT:CMT + 16], IDT,
                        is_transpose=True, start=True, stop=True), inc="te")
                    c.wait("ve", "te")
                    vop( lambda E, j=j, tb=tb: E.tensor_copy(
                        A2[:16, tb * 512 + j * 128:tb * 512 + j * 128 + 128],
                        B[7][:16, 0:128]), inc="ve", drain=True)
            c.wait("gp", "ve")
            c.op("gp", lambda E: E.dma_start(out=cmbD.ap(),
                                             in_=A2[:16, 0:1024]),
                 inc="dg", inck=16)
            return c.cnt["dg"]

        def ffn_experts(xh, xl, cmb_dg, tap0=None):
            c.wait("sy", "dg", cmb_dg)
            c.wait("sy", "te")
            c.wait("sy", "ve")
            c.wait("ve", "te")
            c.wait("ve", "se")
            c.wait("ve", "dg", cmb_dg)
            c.op("ve", lambda E: E.memset(A2[:], 0.0), inc="ve")
            slot_free = [0, 0]
            split_done = [0, 0]
            stage_split = [0]
            dslot_free = [0, 0]
            prev_cb = 0
            for e in range(FFN_NE):
                c.wait("sy", "te", prev_cb)
                c.op("sy", lambda E, e=e: E.dma_start(
                    out=A6[:1, O_CROW:O_CROW + TC], in_=cmbD.ap()[e]),
                    inc="cr", inck=16)
                c.wait("te", "cr")
                c.wait("te", "ve")
                for tb in range(2):
                    c.op("te", lambda E, tb=tb: E.matmul(
                        B[4 + tb][:, 0:512], ONR,
                        A6[:1, O_CROW + tb * 512:O_CROW + tb * 512 + 512],
                        start=True, stop=True), inc="te")
                prev_cb = c.cnt["te"]

                loads = []
                for j in range(FC):
                    loads.append(j * 128)
                    loads.append(FI + j * 128)
                ld_done = [None] * len(loads)

                def issue_gu_load(li, e=e, loads=loads, ld_done=ld_done):
                    off = loads[li]
                    sl = li % 2
                    base = 2048 + sl * 2048
                    c.wait("sy", "ve", stage_split[0])
                    c.op("sy", lambda E, e=e, off=off: E.dma_start(
                        out=ap3(A4, 0, [[6144, 128], [128, DC], [1, 128]]),
                        in_=ap3(din["gu"], e * D * 2 * FI + off,
                                [[2 * FI, 128], [128 * 2 * FI, DC],
                                 [1, 128]])), inc="g0", inck=16)
                    c.wait("ve", "g0")
                    c.wait("ve", "te", slot_free[sl])
                    c.op("ve", lambda E, base=base: E.tensor_copy(
                        bfv(A4, base, 1024), A4[:, 0:2048]), drain=True)
                    c.op("ve", lambda E, base=base: E.tensor_sub(
                        bfv(A4, base + 1024, 1024), A4[:, 0:2048],
                        bfv(A4, base, 1024)), inc="ve", drain=True)
                    stage_split[0] = c.cnt["ve"]
                    split_done[sl] = c.cnt["ve"]
                    ld_done[li] = ("ve", c.cnt["ve"])

                if not FFN_GATE:
                    continue
                for li in range(2):
                    issue_gu_load(li)

                def mm3(bank, sl, k, tb, start, stop):
                    base = 2048 + sl * 2048
                    wh = bfv(A4, base + k * 64, 64)
                    wl = bfv(A4, base + 1024 + k * 64, 64)
                    trips = ((wh, xh(k, tb)), (wh, xl(k, tb)),
                             (wl, xh(k, tb)))
                    for p, (wv, xv) in enumerate(trips):
                        st_ = start and p == 0
                        sp_ = stop and p == 2
                        c.op("te", lambda E, wv=wv, xv=xv, st_=st_, sp_=sp_,
                             bank=bank: E.matmul(
                                 B[bank][:, 0:512], wv, xv,
                                 start=st_, stop=sp_),
                             inc="te" if sp_ else None)

                for j in range(FC):
                    gsl = (2 * j) % 2
                    usl = (2 * j + 1) % 2
                    c.wait("te", *ld_done[2 * j])
                    c.wait("te", "se")
                    for tb in range(2):
                        for k in range(DC):
                            mm3(0 + tb, gsl, k, tb, k == 0, k == DC - 1)
                    gate_done = c.cnt["te"]
                    slot_free[gsl] = gate_done
                    if 2 * j + 2 < len(loads):
                        issue_gu_load(2 * j + 2)
                    c.wait("se", "te", gate_done)
                    c.wait("se", "ve")
                    for tb in range(2):
                        so = SILU0 if tb == 0 else SILU1
                        c.op("se", lambda E, tb=tb, so=so: E.activation(
                            A3[:, so:so + 512], B[0 + tb][:, 0:512],
                            AF.Silu), inc="se")
                    silu_done = c.cnt["se"]
                    c.wait("te", *ld_done[2 * j + 1])
                    c.wait("te", "ve")
                    for tb in range(2):
                        for k in range(DC):
                            mm3(2 + tb, usl, k, tb, k == 0, k == DC - 1)
                    up_done = c.cnt["te"]
                    slot_free[usl] = up_done
                    if 2 * j + 3 < len(loads):
                        issue_gu_load(2 * j + 3)
                    c.wait("ve", "se", silu_done)
                    c.wait("ve", "te", up_done)
                    for tb in range(2):
                        hc = j * 1024 + tb * 512
                        so = SILU0 if tb == 0 else SILU1
                        c.op("ve", lambda E, tb=tb, hc=hc, so=so:
                             E.scalar_tensor_tensor(
                                 A3[:, hc:hc + 512], B[2 + tb][:, 0:512],
                                 1.0, A3[:, so:so + 512], MUL, MUL),
                             drain=True)
                        c.op("ve", lambda E, tb=tb, hc=hc: E.tensor_mul(
                            A3[:, hc:hc + 512], A3[:, hc:hc + 512],
                            B[4 + tb][:, 0:512]), inc="ve")
                h_done = c.cnt["ve"]

                def issue_down(dd, e=e):
                    sl = dd % 2
                    c.wait("sy", "te", dslot_free[sl])
                    c.op("sy", lambda E, e=e, dd=dd, sl=sl: E.dma_start(
                        out=ap3(A5, sl * 768,
                                [[1536, 128], [128, FC], [1, 128]]),
                        in_=ap3(din["dwn"], e * FI * D + dd * 128,
                                [[D, 128], [128 * D, FC], [1, 128]])),
                        inc=f"d{sl}", inck=16)

                if FFN_DOWN >= 1:
                    issue_down(0)
                    issue_down(1)
                for dd in range(DC if FFN_DOWN >= 1 else 0):
                    sl = dd % 2
                    if FFN_DOWN < 2:
                        if dd + 2 < DC:
                            issue_down(dd + 2)
                        continue
                    c.wait("te", f"d{sl}")
                    if dd == 0:
                        c.wait("te", "ve", h_done)
                    c.wait("te", "ve")
                    for tb in range(2):
                        bk = 6 + tb
                        for f in range(FC):
                            c.op("te", lambda E, tb=tb, f=f, sl=sl, bk=bk:
                                 E.matmul(
                                     B[bk][:, 0:512],
                                     A5[:, sl * 768 + f * 128:
                                        sl * 768 + f * 128 + 128],
                                     A3[:, f * 1024 + tb * 512:
                                        f * 1024 + tb * 512 + 512],
                                     start=(f == 0), stop=(f == FC - 1)),
                                 inc="te" if f == FC - 1 else None)
                        mm_dn = c.cnt["te"]
                        if FFN_DOWN >= 3:
                            c.wait("ve", "te", mm_dn)
                            ac = dd * 1024 + tb * 512
                            c.op("ve", lambda E, ac=ac, bk=bk: E.tensor_add(
                                A2[:, ac:ac + 512], A2[:, ac:ac + 512],
                                B[bk][:, 0:512]), inc="ve")
                    dslot_free[sl] = c.cnt["te"]
                    if dd + 2 < DC:
                        issue_down(dd + 2)
                if e == 0 and tap0 is not None:
                    c.wait("gp", "ve")
                    c.op("gp", lambda E: E.dma_start(
                        out=ap3(tap0, 0, [[TC, 128], [128 * TC, DC],
                                          [1, TC]]),
                        in_=ap3(A2, 0, [[16384, 128], [TC, DC], [1, TC]])),
                        inc="dg", inck=16)
                    c.wait("ve", "dg", c.cnt["dg"])

        def residual_out(base_dram, dst_dram):
            c.wait("gp", "dg")
            for half in range(2):
                c.wait("gp", "ve")
                c.wait("gp", "te")
                c.op("gp", lambda E, half=half: E.dma_start(
                    out=ap3(A1, half * 8192,
                            [[16384, 128], [TC, 8], [1, TC]]),
                    in_=ap3(base_dram, half * 8 * 128 * TC,
                            [[TC, 128], [128 * TC, 8], [1, TC]])),
                    inc="dg", inck=16)
                ld = c.cnt["dg"]
                c.wait("ve", "dg", ld)
                c.wait("ve", "te")
                for kk in range(8):
                    k = half * 8 + kk
                    ao = half * 8192 + kk * TC
                    c.op("ve", lambda E, k=k, ao=ao:
                         E.scalar_tensor_tensor(
                             A1[:, ao:ao + TC], A2[:, k * TC:k * TC + TC],
                             0.5, A1[:, ao:ao + TC], MUL, ADD), inc="ve")
            c.wait("gp", "ve")
            c.op("gp", lambda E: E.dma_start(
                out=ap3(dst_dram, 0, [[TC, 128], [128 * TC, DC], [1, TC]]),
                in_=ap3(A1, 0, [[16384, 128], [TC, DC], [1, TC]])),
                inc="dg", inck=16)

        def emit_av(kt, se_exp, te_av):
            p = kt % 2
            c.wait("te", "se", se_exp[kt])
            st, sp = (kt == 0), (kt == 31)
            e1 = EOF + p * 512
            e2 = EOF + (2 + p) * 512
            c.op("te", lambda E, kt=kt, e1=e1, st=st, sp=sp: E.matmul(
                B[4][:, 0:512], A4[:, kt * 128:kt * 128 + 128],
                A2[:, e1:e1 + 512], start=st, stop=sp))
            c.op("te", lambda E, kt=kt, e2=e2, st=st, sp=sp: E.matmul(
                B[5][:, 0:512], A4[:, kt * 128:kt * 128 + 128],
                A2[:, e2:e2 + 512], start=st, stop=sp))
            c.op("te", lambda E, e1=e1, st=st, sp=sp: E.matmul(
                B[6][:1, 0:512], ONESC, A2[:, e1:e1 + 512],
                start=st, stop=sp))
            c.op("te", lambda E, e2=e2, st=st, sp=sp: E.matmul(
                B[7][:1, 0:512], ONESC, A2[:, e2:e2 + 512],
                start=st, stop=sp), inc="te")
            te_av[kt] = c.cnt["te"]

        def _schedule():
            if PL < 1:
                raise _PhaseStop()
            for tb in range(2):
                ln_block(lambda k, tb=tb: A2[:, k * TC + tb * 512:
                                             k * TC + tb * 512 + 512],
                         None, 512,
                         split=(LN_STAGE,
                                (lambda k, tb=tb: xh_t(k, tb)),
                                (lambda k, tb=tb: xl_t(k, tb))))
            if PL < 2:
                raise _PhaseStop()
            cmb_dg = router(xh_t, xl_t)
            if DEBUG_TAPS:
                c.wait("gp", "ve")
                c.wait("gp", "te")
                c.op("gp", lambda E: E.dma_start(
                    out=ap3(dbg["dbg_xa"], 0,
                            [[TC, 128], [128 * TC, DC], [1, TC]]),
                    in_=ap3(A1, 0, [[16384, 128], [TC, DC], [1, TC]])),
                    inc="dg", inck=16)
                tap_xa_dg = c.cnt["dg"]
                c.wait("te", "dg", tap_xa_dg)
            if PL < 3:
                raise _PhaseStop()
            ffn_experts(xh_t, xl_t, cmb_dg, tap0=dbg.get("dbg_acc0"))
            if DEBUG_TAPS:
                c.wait("gp", "ve")
                c.op("gp", lambda E: E.dma_start(out=dbg["dbg_cmb"].ap(),
                                                 in_=cmbD.ap()),
                     inc="dg", inck=16)
                c.op("gp", lambda E: E.dma_start(
                    out=ap3(dbg["dbg_acc"], 0,
                            [[TC, 128], [128 * TC, DC], [1, TC]]),
                    in_=ap3(A2, 0, [[16384, 128], [TC, DC], [1, TC]])),
                    inc="dg", inck=16)
                c.op("gp", lambda E: E.dma_start(
                    out=ap3(dbg["dbg_xln1"], 0,
                            [[TC, 128], [128 * TC, DC], [1, TC]]),
                    in_=ap3(A1, 0, [[16384, 128], [TC, DC], [1, TC]])),
                    inc="dg", inck=16)
                c.op("gp", lambda E: E.dma_start(
                    out=ap3(dbg["dbg_h"], 0, [[6144, 128], [1, 6144]]),
                    in_=ap3(A3, 0, [[8192, 128], [1, 6144]])),
                    inc="dg", inck=16)
            if PL < 4:
                raise _PhaseStop()
            residual_out(din["xT"], y1b)
            y1_st = c.cnt["dg"]

            if PL < 5:
                raise _PhaseStop()
            c.wait("gp", "dg", y1_st)
            c.op("gp", lambda E: E.collective_compute(
                "AllGather", mybir.AluOpType.bypass,
                replica_groups=[[0, 1, 2, 3], [4, 5, 6, 7]],
                ins=[y1b.ap().opt()], outs=[xg.ap().opt()]), inc="cc")

            if PL < 6:
                raise _PhaseStop()
            c.wait("sy", "cc")
            st_done = [0, 0]
            for blk in range(8):
                r, hf = blk // 2, blk % 2
                sl = blk % 2
                c.wait("sy", "dg", st_done[sl])
                c.wait("sy", "ve")
                c.wait("sy", "te")
                c.op("sy", lambda E, r=r, hf=hf, sl=sl: E.dma_start(
                    out=ap3(A1, sl * 8192, [[16384, 128], [512, DC], [1, 512]]),
                    in_=ap3(xg, r * D * TC + hf * 512,
                            [[TC, 128], [128 * TC, DC], [1, 512]])),
                    inc="dw", inck=16)
                ld = c.cnt["dw"]
                c.wait("ve", "dw", ld)
                ln_block(lambda k, sl=sl: A1[:, sl * 8192 + k * 512:
                                             sl * 8192 + k * 512 + 512],
                         lambda k, sl=sl: A1[:, sl * 8192 + k * 512:
                                             sl * 8192 + k * 512 + 512], 512)
                c.wait("gp", "ve")
                c.op("gp", lambda E, r=r, hf=hf, sl=sl: E.dma_start(
                    out=ap3(xlnD, r * D * TC + hf * 512,
                            [[TC, 128], [128 * TC, DC], [1, 512]]),
                    in_=ap3(A1, sl * 8192, [[16384, 128], [512, DC], [1, 512]])),
                    inc="dg", inck=16)
                st_done[sl] = c.cnt["dg"]
            ln2_st = c.cnt["dg"]

            if PL < 7:
                raise _PhaseStop()
            for hh in range(min(4, PL - 6)):
                c.wait("sy", "te")
                c.wait("sy", "ve")
                c.wait("sy", "se")
                if hh == 0:
                    c.wait("sy", "dg", ln2_st)
                    c.wait("ve", "dg", ln2_st)
                wdsts = (ap3(A4, 4096, [[6144, 128], [128, DC], [1, 128]]),
                         ap3(A3, 4096, [[8192, 128], [128, DC], [1, 128]]),
                         ap3(A3, 6144, [[8192, 128], [128, DC], [1, 128]]))
                for nm, dst in zip(("wq4", "wk4", "wv4"), wdsts):
                    c.op("sy", lambda E, nm=nm, dst=dst, hh=hh: E.dma_start(
                        out=dst,
                        in_=ap3(din[nm], hh * 128,
                                [[512, 128], [128 * 512, DC], [1, 128]])),
                        inc="dw", inck=16)
                w_ld = c.cnt["dw"]

                tile_free = [0, 0]
                for tbt in range(32):
                    r, pos = tbt // 8, tbt % 8
                    sl = tbt % 2
                    c.wait("sy", "te", tile_free[sl])
                    c.op("sy", lambda E, r=r, pos=pos, sl=sl: E.dma_start(
                        out=ap3(A3, sl * 2048,
                                [[8192, 128], [128, DC], [1, 128]]),
                        in_=ap3(xlnD, r * D * TC + pos * 128,
                                [[TC, 128], [128 * TC, DC], [1, 128]])),
                        inc="dw", inck=16)
                    ld = c.cnt["dw"]
                    c.wait("te", "dw", max(ld, w_ld))
                    c.wait("te", "ve")
                    wsrc = ((A4, 4096), (A3, 4096), (A3, 6144))
                    for pi in range(3):
                        wh, wo_ = wsrc[pi]
                        for k in range(DC):
                            c.op("te", lambda E, pi=pi, k=k, sl=sl, wh=wh,
                                 wo_=wo_: E.matmul(
                                     B[pi][:, 0:128],
                                     wh[:, wo_ + k * 128:wo_ + k * 128 + 128],
                                     A3[:, sl * 2048 + k * 128:
                                        sl * 2048 + k * 128 + 128],
                                     start=(k == 0), stop=(k == DC - 1)),
                                 inc="te" if k == DC - 1 else None)
                    qkv_done = c.cnt["te"]
                    tile_free[sl] = qkv_done
                    c.wait("ve", "te", qkv_done)
                    c.op("ve", lambda E, tbt=tbt: E.tensor_copy(
                        A2[:, QOF + tbt * 128:QOF + tbt * 128 + 128],
                        B[0][:, 0:128]))
                    c.op("ve", lambda E, tbt=tbt: E.tensor_copy(
                        A2[:, KOF + tbt * 128:KOF + tbt * 128 + 128],
                        B[1][:, 0:128]))
                    c.op("ve", lambda E: E.tensor_copy(
                        A2[:, VST:VST + 128], B[2][:, 0:128]), inc="ve")
                    vs_done = c.cnt["ve"]
                    c.wait("te", "ve", vs_done)
                    c.op("te", lambda E: E.matmul(
                        B[3][:, 0:128], A2[:, VST:VST + 128], IDT,
                        is_transpose=True, start=True, stop=True), inc="te")
                    c.wait("ve", "te")
                    c.op("ve", lambda E, tbt=tbt: E.tensor_copy(
                        A4[:, tbt * 128:tbt * 128 + 128], B[3][:, 0:128]),
                        inc="ve")

                def bound(off, qn_off):
                    c.wait("se", "ve")
                    c.wait("se", "te")
                    c.op("se", lambda E, off=off: E.activation(
                        A3[:, 4096:8192], A2[:, off:off + 4096], AF.Square),
                        inc="se")
                    sq = c.cnt["se"]
                    c.wait("te", "se", sq)
                    for blk in range(8):
                        c.wait("te", "ve")
                        c.op("te", lambda E, blk=blk: E.matmul(
                            B[5][:1, 0:512], A6[:, O_O64A:O_O64A + 1],
                            A3[:, 4096 + blk * 512:4096 + blk * 512 + 512],
                            start=True, stop=True))
                        c.op("te", lambda E, blk=blk: E.matmul(
                            B[6][:1, 0:512], A6[:, O_O64B:O_O64B + 1],
                            A3[:, 4096 + blk * 512:4096 + blk * 512 + 512],
                            start=True, stop=True), inc="te")
                        mmv = c.cnt["te"]
                        c.wait("ve", "te", mmv)
                        for hi, bk in ((0, 5), (1, 6)):
                            qo = qn_off + hi
                            if blk == 0:
                                c.op("ve", lambda E, bk=bk, qo=qo: E.reduce_max(
                                    A6[:1, qo:qo + 1], B[bk][:1, 0:512],
                                    axis=AX))
                            else:
                                c.op("ve", lambda E, bk=bk: E.reduce_max(
                                    A6[:1, O_SML:O_SML + 1], B[bk][:1, 0:512],
                                    axis=AX))
                                c.op("ve", lambda E, qo=qo: E.tensor_max(
                                    A6[:1, qo:qo + 1], A6[:1, qo:qo + 1],
                                    A6[:1, O_SML:O_SML + 1]))
                        c.op("ve", lambda E: E.tensor_copy(
                            A6[:1, O_SML + 1:O_SML + 2],
                            A6[:1, O_SML:O_SML + 1]), inc="ve")

                bound(QOF, O_QN)
                bound(KOF, O_QN + 2)
                c.wait("se", "ve")
                c.op("se", lambda E: E.activation(
                    A6[:1, O_QN:O_QN + 4], A6[:1, O_QN:O_QN + 4], AF.Sqrt),
                    inc="se")
                c.wait("ve", "se")
                c.op("ve", lambda E: E.tensor_mul(
                    A6[:1, O_QN + 4:O_QN + 6], A6[:1, O_QN:O_QN + 2],
                    A6[:1, O_QN + 2:O_QN + 4]))
                c.op("ve", lambda E: E.tensor_scalar(
                    A6[:1, O_QN + 4:O_QN + 6], A6[:1, O_QN + 4:O_QN + 6],
                    -SCALE, None, MUL), inc="ve")
                c.wait("te", "ve")
                c.op("te", lambda E: E.matmul(
                    B[4][:, 0:1], ONR, A6[:1, O_QN + 4:O_QN + 5],
                    start=True, stop=True))
                c.op("te", lambda E: E.matmul(
                    B[5][:, 0:1], ONR, A6[:1, O_QN + 5:O_QN + 6],
                    start=True, stop=True), inc="te")
                c.wait("ve", "te")
                c.op("ve", lambda E: E.tensor_copy(A6[:, O_NC1:O_NC1 + 1],
                                                   B[4][:, 0:1]))
                c.op("ve", lambda E: E.tensor_copy(A6[:, O_NC2:O_NC2 + 1],
                                                   B[5][:, 0:1]), inc="ve")

                for qb in range(8):
                    se_exp = {}
                    te_av = {}
                    for kt in range(32):
                        p = kt % 2
                        if kt == 0:
                            c.wait("te", "ve")
                        if kt >= 2:
                            c.wait("te", "se", se_exp[kt - 2])
                        for half in range(2):
                            bnk = (0 if half == 0 else 2) + p
                            po = half * 64
                            c.op("te", lambda E, kt=kt, po=po, bnk=bnk, qb=qb:
                                 E.matmul(
                                     B[bnk][:, 0:512],
                                     A2[po:po + 64,
                                        KOF + kt * 128:KOF + kt * 128 + 128],
                                     A2[po:po + 64,
                                        QOF + qb * 512:QOF + qb * 512 + 512],
                                     start=True, stop=True),
                                 inc="te" if half == 1 else None)
                        te_scr = c.cnt["te"]
                        if kt >= 1:
                            emit_av(kt - 1, se_exp, te_av)
                        c.wait("se", "te", te_scr)
                        if kt >= 2:
                            c.wait("se", "te", te_av.get(kt - 2, 0))
                        for half in range(2):
                            bnk = (0 if half == 0 else 2) + p
                            nco = O_NC1 if half == 0 else O_NC2
                            eo = EOF + (half * 2 + p) * 512
                            c.op("se", lambda E, bnk=bnk, nco=nco, eo=eo:
                                 E.activation(
                                     A2[:, eo:eo + 512], B[bnk][:, 0:512],
                                     AF.Exp, bias=A6[:, nco:nco + 1],
                                     scale=SCALE),
                                 inc="se" if half == 1 else None)
                        se_exp[kt] = c.cnt["se"]
                    emit_av(31, se_exp, te_av)
                    c.wait("ve", "te", te_av[31])
                    c.op("ve", lambda E: E.reciprocal(A6[:1, O_ROW:O_ROW + 512],
                                                      B[6][:1, 0:512]))
                    c.op("ve", lambda E: E.reciprocal(
                        A6[:1, O_ROW2:O_ROW2 + 512], B[7][:1, 0:512]))
                    c.op("ve", lambda E: E.tensor_scalar(
                        A6[:1, O_ROW2:O_ROW2 + 512],
                        A6[:1, O_ROW2:O_ROW2 + 512],
                        A6[:1, O_LAM:O_LAM + 1], None, MUL), inc="ve")
                    c.wait("te", "ve")
                    c.op("te", lambda E: E.matmul(
                        B[0][:, 0:512], ONR, A6[:1, O_ROW:O_ROW + 512],
                        start=True, stop=True))
                    c.op("te", lambda E: E.matmul(
                        B[1][:, 0:512], ONR, A6[:1, O_ROW2:O_ROW2 + 512],
                        start=True, stop=True), inc="te")
                    c.wait("ve", "te")
                    c.op("ve", lambda E: E.tensor_copy(A3[:, 4096:4608],
                                                       B[4][:, 0:512]))
                    c.op("ve", lambda E: E.tensor_copy(A3[:, 4608:5120],
                                                       B[5][:, 0:512]))
                    c.op("ve", lambda E: E.tensor_mul(A3[:, 4096:4608],
                                                      A3[:, 4096:4608],
                                                      B[0][:, 0:512]))
                    c.op("ve", lambda E: E.tensor_mul(A3[:, 4608:5120],
                                                      A3[:, 4608:5120],
                                                      B[1][:, 0:512]))
                    c.op("ve", lambda E, qb=qb: E.tensor_sub(
                        A2[:, OOF + qb * 512:OOF + qb * 512 + 512],
                        A3[:, 4096:4608], A3[:, 4608:5120]), inc="ve")

                c.wait("se", "ve")
                c.wait("se", "te")
                c.op("se", lambda E: E.activation(A3[:, 4096:8192],
                                                  A2[:, OOF:OOF + 4096],
                                                  AF.Square), inc="se")
                sq_rms = c.cnt["se"]
                c.wait("te", "se", sq_rms)
                for blk in range(8):
                    c.wait("te", "ve")
                    c.op("te", lambda E, blk=blk: E.matmul(
                        B[6][:1, 0:512], ONESC,
                        A3[:, 4096 + blk * 512:4096 + blk * 512 + 512],
                        start=True, stop=True), inc="te")
                    mm_s = c.cnt["te"]
                    c.wait("ve", "te", mm_s)
                    c.op("ve", lambda E: E.tensor_scalar(
                        A6[:1, O_ROW:O_ROW + 512], B[6][:1, 0:512], 1.0 / 128,
                        None, MUL), inc="ve")
                    r_ve = c.cnt["ve"]
                    c.wait("se", "ve", r_ve)
                    c.op("se", lambda E: E.activation(
                        A6[:1, O_ROW:O_ROW + 512], A6[:1, O_ROW:O_ROW + 512],
                        AF.Sqrt, bias=A6[:1, O_EPS2:O_EPS2 + 1]), inc="se")
                    c.wait("ve", "se")
                    c.op("ve", lambda E: E.reciprocal(
                        A6[:1, O_ROW:O_ROW + 512], A6[:1, O_ROW:O_ROW + 512]))
                    c.op("ve", lambda E: E.tensor_scalar(
                        A6[:1, O_ROW:O_ROW + 512], A6[:1, O_ROW:O_ROW + 512],
                        1.0 - LAMBDA_INIT, None, MUL), inc="ve")
                    c.wait("te", "ve")
                    c.op("te", lambda E: E.matmul(
                        B[4][:, 0:512], ONR, A6[:1, O_ROW:O_ROW + 512],
                        start=True, stop=True), inc="te")
                    c.wait("ve", "te")
                    c.op("ve", lambda E, blk=blk: E.tensor_mul(
                        A2[:, OOF + blk * 512:OOF + blk * 512 + 512],
                        A2[:, OOF + blk * 512:OOF + blk * 512 + 512],
                        B[4][:, 0:512]), inc="ve")

                for lp in range(32):
                    s_hi, lo = lp // 16, lp % 16
                    cpr = 2 * hh + s_hi
                    base = OOF + s_hi * 64 + 4 * lo
                    c.op("ve", lambda E, base=base: E.tensor_copy(
                        A2[:, VST:VST + 128],
                        ap3(A2, base, [[16384, 128], [1, 4], [1024, 4],
                                       [128, 8]])), inc="ve")
                    g_done = c.cnt["ve"]
                    c.wait("te", "ve", g_done)
                    bnk = 2 + lp % 2
                    c.op("te", lambda E, bnk=bnk: E.matmul(
                        B[bnk][:, 0:128], A2[:, VST:VST + 128], IDT,
                        is_transpose=True, start=True, stop=True), inc="te")
                    c.wait("ve", "te")
                    c.op("ve", lambda E, lo=lo, cpr=cpr, bnk=bnk: E.tensor_copy(
                        A1[:, lo * 1024 + cpr * 128:
                           lo * 1024 + cpr * 128 + 128], B[bnk][:, 0:128]),
                        inc="ve")

            if PL < 11:
                raise _PhaseStop()
            c.wait("sy", "te")
            c.wait("sy", "ve")
            c.wait("sy", "se")
            wo_free = [0, 0]
            res_free = [0, 0]
            for m in range(DC):
                wsl = m % 2
                wdst = (ap3(A4, 4096, [[6144, 128], [128, DC], [1, 128]])
                        if wsl == 0 else
                        ap3(A3, 4096, [[8192, 128], [128, DC], [1, 128]]))
                c.wait("sy", "te", wo_free[wsl])
                c.op("sy", lambda E, m=m, wdst=wdst: E.dma_start(
                    out=wdst,
                    in_=ap3(din["woT"], m * 128,
                            [[D, 128], [128 * D, DC], [1, 128]])),
                    inc="dw", inck=16)
                w_ld = c.cnt["dw"]
                rsl = m % 2
                c.wait("sy", "ve", res_free[rsl])
                c.op("sy", lambda E, m=m, rsl=rsl: E.dma_start(
                    out=ap3(A3, 6144 + rsl * 1024, [[8192, 128], [1, TC]]),
                    in_=ap3(y1b, m * 128 * TC, [[TC, 128], [1, TC]])),
                    inc="dw", inck=16)
                r_ld = c.cnt["dw"]
                for jb in range(2):
                    bnk = jb
                    c.wait("te", "dw", w_ld)
                    c.wait("te", "ve")
                    for lo in range(DC):
                        if wsl == 0:
                            wap = A4[:, 4096 + lo * 128:4096 + lo * 128 + 128]
                        else:
                            wap = A3[:, 4096 + lo * 128:4096 + lo * 128 + 128]
                        c.op("te", lambda E, lo=lo, jb=jb, bnk=bnk, wap=wap:
                             E.matmul(
                                 B[bnk][:, 0:512], wap,
                                 A1[:, lo * 1024 + jb * 512:
                                    lo * 1024 + jb * 512 + 512],
                                 start=(lo == 0), stop=(lo == DC - 1)),
                             inc="te" if lo == DC - 1 else None)
                    mm_done = c.cnt["te"]
                    c.wait("ve", "te", mm_done)
                    c.wait("ve", f"r{rsl}")
                    c.op("ve", lambda E, m=m, jb=jb, bnk=bnk, rsl=rsl:
                         E.scalar_tensor_tensor(
                             A2[:, m * TC + jb * 512:m * TC + jb * 512 + 512],
                             B[bnk][:, 0:512], 1.0,
                             A3[:, 6144 + rsl * 1024 + jb * 512:
                                6144 + rsl * 1024 + jb * 512 + 512],
                             MUL, ADD), inc="ve")
                wo_free[wsl] = c.cnt["te"]
                res_free[rsl] = c.cnt["ve"]
                c.wait("gp", "ve")
                c.op("gp", lambda E, m=m: E.dma_start(
                    out=ap3(x2b, m * 128 * TC, [[TC, 128], [1, TC]]),
                    in_=A2[:, m * TC:m * TC + TC]), inc="dg", inck=16)

            if PL < 12:
                raise _PhaseStop()
            for tb in range(2):
                ln_block(lambda k, tb=tb: A2[:, k * TC + tb * 512:
                                             k * TC + tb * 512 + 512],
                         None, 512,
                         split=(LN_STAGE,
                                (lambda k, tb=tb: xh_t(k, tb)),
                                (lambda k, tb=tb: xl_t(k, tb))))
            if PL < 13:
                raise _PhaseStop()
            cmb_dg2 = router(xh_t, xl_t)
            if PL < 14:
                raise _PhaseStop()
            ffn_experts(xh_t, xl_t, cmb_dg2)
            if PL < 15:
                raise _PhaseStop()
            residual_out(x2b, outT)

        try:
            _schedule()
        except _PhaseStop:
            c.wait("gp", "ve")
            c.wait("gp", "te")
            c.wait("gp", "se")
            c.wait("gp", "dw")
            c.op("gp", lambda E: E.dma_start(
                out=ap3(outT, 0, [[TC, 128], [128 * TC, DC], [1, TC]]),
                in_=ap3(A1, 0, [[16384, 128], [TC, DC], [1, TC]])),
                inc="dg", inck=16)

        _phase_stopped = False
        if DEBUG_TAPS:
            c.wait("gp", "dg")
            c.wait("gp", "cc")
            c.op("gp", lambda E: E.dma_start(out=dbg["dbg_y1"].ap(),
                                             in_=y1b.ap()),
                 inc="dg", inck=16)
            c.op("gp", lambda E: E.dma_start(out=dbg["dbg_xg"].ap(),
                                             in_=xg.ap()),
                 inc="dg", inck=16)
            c.op("gp", lambda E: E.dma_start(out=dbg["dbg_xln"].ap(),
                                             in_=xlnD.ap()),
                 inc="dg", inck=16)
            c.op("gp", lambda E: E.dma_start(out=dbg["dbg_x2"].ap(),
                                             in_=x2b.ap()),
                 inc="dg", inck=16)
        c.wait("gp", "dg")

        with nc.Block() as blk:
            @blk.tensor
            def _(E):
                for f in c.ops["te"]:
                    f(E)

            @blk.vector
            def _(E):
                for f in c.ops["ve"]:
                    f(E)

            @blk.scalar
            def _(E):
                for f in c.ops["se"]:
                    f(E)

            @blk.sync
            def _(E):
                for f in c.ops["sy"]:
                    f(E)

            @blk.gpsimd
            def _(E):
                for f in c.ops["gp"]:
                    f(E)

    return nc


def _mk_host_constants():
    ones64a = np.zeros((128, 1), np.float32)
    ones64a[:64] = 1.0
    ones64b = np.zeros((128, 1), np.float32)
    ones64b[64:] = 1.0
    return {
        "id128": np.eye(128, dtype=np.float32),
        "ones128": np.ones((128, 1), np.float32),
        "onesr": np.ones((1, 128), np.float32),
        "ones64a": ones64a,
        "ones64b": ones64b,
        "epsln": np.full((128, 1), LN_EPS, np.float32),
        "epsrms": np.full((128, 1), RMS_EPS, np.float32),
    }


def _token_perm(g):
    j = np.arange(TC)
    e = j % 128
    cpr = j // 128
    hprime = cpr // 2
    s_hi = cpr % 2
    return 32 * e + 8 * g + 2 * hprime + s_hi


def kernel(L0_attn_x, L0_attn_wq, L0_attn_wk, L0_attn_wv, L0_attn_wo,
           L0_attn_lambda_q1, L0_attn_lambda_q2, L0_attn_lambda_k1,
           L0_attn_lambda_k2, L0_ffn_router_weight, L0_ffn_gate_up_proj,
           L0_ffn_down_proj):
    x = np.asarray(L0_attn_x, np.float32)
    wq = np.asarray(L0_attn_wq, np.float32)
    wk = np.asarray(L0_attn_wk, np.float32)
    wv = np.asarray(L0_attn_wv, np.float32)
    wo = np.asarray(L0_attn_wo, np.float32)
    rw = np.asarray(L0_ffn_router_weight, np.float32)
    gup = np.ascontiguousarray(np.asarray(L0_ffn_gate_up_proj, np.float32))
    dwn = np.ascontiguousarray(np.asarray(L0_ffn_down_proj, np.float32))
    consts = _mk_host_constants()

    wqT = np.ascontiguousarray(wq.T)
    wkT = np.ascontiguousarray(wk.T)
    wvT = np.ascontiguousarray(wv.T)
    woT = np.ascontiguousarray(wo.T)
    rwT = np.ascontiguousarray(rw.T)

    def chunks(a, ncol):
        return np.ascontiguousarray(a.reshape(DC, 128, ncol))

    in_maps = []
    for core in range(8):
        bc, g = core // 4, core % 4
        tp = _token_perm(g)
        xTc = np.ascontiguousarray(x[bc].T[:, tp])
        m = {
            "xT": chunks(xTc, TC),
            "rwT": chunks(rwT, NE),
            "gu": gup,
            "dwn": dwn,
            "wq4": chunks(np.ascontiguousarray(
                wqT[:, 512 * g:512 * g + 512]), 512),
            "wk4": chunks(np.ascontiguousarray(
                wkT[:, 512 * g:512 * g + 512]), 512),
            "wv4": chunks(np.ascontiguousarray(
                wvT[:, 512 * g:512 * g + 512]), 512),
            "woT": chunks(woT, D),
            "lq1": np.asarray(L0_attn_lambda_q1, np.float32).reshape(128, 1),
            "lq2": np.asarray(L0_attn_lambda_q2, np.float32).reshape(128, 1),
            "lk1": np.asarray(L0_attn_lambda_k1, np.float32).reshape(128, 1),
            "lk2": np.asarray(L0_attn_lambda_k2, np.float32).reshape(128, 1),
        }
        m.update(consts)
        in_maps.append(m)

    nc = build_program()
    res = run_bass_kernel_spmd(nc, in_maps, core_ids=list(range(8)))

    out = np.empty((2, S, D), np.float32)
    for core in range(8):
        bc, g = core // 4, core % 4
        tp = _token_perm(g)
        oT = np.asarray(res.results[core]["outT"]).reshape(D, TC)
        out[bc, tp, :] = oT.T
    return out


# version 07
# speedup vs baseline: 1.1159x; 1.0229x over previous
import contextlib
import numpy as np
import concourse.bass as bass
import concourse.mybir as mybir
from concourse.bass_utils import run_bass_kernel_spmd
from concourse.alu_op_type import AluOpType

F32 = mybir.dt.float32
BF16 = mybir.dt.bfloat16
AF = mybir.ActivationFunctionType
AX = mybir.AxisListType.X
ADD, SUB, MUL = AluOpType.add, AluOpType.subtract, AluOpType.mult

D = 2048
DC = 16
TC = 1024
S = 4096
NE = 16
FI = 768
FC = 6
SCALE = float(1.0 / np.sqrt(np.float32(128.0)))
LN_EPS = 1e-5
RMS_EPS = 1.1920928955078125e-07
LAMBDA_INIT = 0.8

ENGS = ("te", "ve", "se", "sy", "gp")
DEBUG_TAPS = False
PHASE_LIMIT = 99
FFN_NE = 16
FFN_GATE = True
FFN_DOWN = 3


class Sched:

    def __init__(self, nc):
        self.nc = nc
        self.ops = {e: [] for e in ENGS}
        self.sem = {}
        self.cnt = {}

    def mksem(self, stack, *names):
        for n in names:
            self.sem[n] = stack.enter_context(self.nc.semaphore(n))
            self.cnt[n] = 0

    def wait(self, eng, sname, val=None):
        v = self.cnt[sname] if val is None else val
        if v <= 0:
            return
        s = self.sem[sname]
        self.ops[eng].append(lambda E: E.wait_ge(s, v))

    def op(self, eng, fn, inc=None, inck=1, drain=False):
        if inc is None:
            self.ops[eng].append(fn)
        else:
            s = self.sem[inc]
            self.ops[eng].append(lambda E: fn(E).then_inc(s, inck))
            self.cnt[inc] += inck
        if drain:
            self.ops[eng].append(lambda E: E.drain())
        return self.cnt[inc] if inc is not None else None


class _PhaseStop(Exception):
    pass


def ap3(h, off, dims):
    t = getattr(h, "tensor", h)
    return bass.AP(t, off, dims)


def build_program(phase_limit=None):
    PL = PHASE_LIMIT if phase_limit is None else phase_limit
    nc = bass.Bass("TRN2", target_bir_lowering=False,
                   detect_race_conditions=False)

    din = {}
    for name, shape in [
            ("xT", [DC, 128, TC]), ("rwT", [DC, 128, NE]),
            ("gu", [NE, D, 2 * FI]), ("dwn", [NE, FI, D]),
            ("wq4", [DC, 128, 512]), ("wk4", [DC, 128, 512]),
            ("wv4", [DC, 128, 512]), ("woT", [DC, 128, D]),
            ("lq1", [128, 1]), ("lq2", [128, 1]), ("lk1", [128, 1]),
            ("lk2", [128, 1]), ("id128", [128, 128]), ("ones128", [128, 1]),
            ("onesr", [1, 128]), ("ones64a", [128, 1]), ("ones64b", [128, 1]),
            ("epsln", [128, 1]), ("epsrms", [128, 1])]:
        din[name] = nc.dram_tensor(name, shape, F32, kind="ExternalInput")
    outT = nc.dram_tensor("outT", [DC, 128, TC], F32, kind="ExternalOutput")
    dbg = {}
    if DEBUG_TAPS:
        for nm, shape in (("dbg_y1", [D, TC]), ("dbg_xg", [4, D, TC]),
                          ("dbg_xln", [4, D, TC]), ("dbg_x2", [D, TC]),
                          ("dbg_cmb", [NE, TC]), ("dbg_acc", [D, TC]),
                          ("dbg_xln1", [D, TC]), ("dbg_h", [128, 6144]),
                          ("dbg_xa", [D, TC]), ("dbg_acc0", [D, TC])):
            dbg[nm] = nc.dram_tensor(nm, shape, F32, kind="ExternalOutput")

    y1b = nc.dram_tensor("y1b", [D, TC], F32)
    xg = nc.dram_tensor("xg", [4, D, TC], F32)
    xlnD = nc.dram_tensor("xlnD", [4, D, TC], F32)
    x2b = nc.dram_tensor("x2b", [D, TC], F32)
    cmbD = nc.dram_tensor("cmbD", [NE, TC], F32)

    stack = contextlib.ExitStack()
    with stack:
        sbuf = lambda name, w: stack.enter_context(
            nc.sbuf_tensor(name, [128, w], F32))
        A1 = sbuf("A1", 16384)
        A2 = sbuf("A2", 16384)
        A3 = sbuf("A3", 8192)
        A4 = sbuf("A4", 6144)
        A5 = sbuf("A5", 1536)
        A6 = sbuf("A6", 2688)
        B = [stack.enter_context(nc.psum_tensor(f"B{i}", [128, 512], F32))
             for i in range(8)]

        c = Sched(nc)
        c.mksem(stack, "te", "ve", "se", "dw", "dg", "cc",
                "g0", "g1", "g2", "d0", "d1", "cr",
                "q0", "q1", "qw", "w0", "w1", "r0", "r1",
                "l0", "l1", "s0", "s1")

        O_ID = 0
        O_ONES = 128
        O_O64A = 129
        O_O64B = 130
        O_ONR = 131
        O_LAM = 259
        O_NC1 = 260
        O_NC2 = 261
        O_LQ = 262
        O_QN = 266
        O_EPS1 = 336
        O_EPS2 = 337
        O_SML = 274
        O_ROW = 338
        O_ROW2 = 850
        O_CROW = 1362
        O_RW = 2386

        ONESC = A6[:, O_ONES:O_ONES + 1]
        ONR = A6[:1, O_ONR:O_ONR + 128]
        IDT = A6[:, O_ID:O_ID + 128]

        SQ0, SQ1 = 0, 512
        SILU0, SILU1 = 6144, 6656
        QOF, KOF, OOF, EOF, VST = 0, 4096, 8192, 12288, 14336

        def bfv(h, col, w):
            return h[:, col:col + w].bitcast(BF16)

        def xh_t(k, tb):
            return bfv(A1, k * 512 + tb * 256, 256)

        def xl_t(k, tb):
            return bfv(A1, 8192 + k * 512 + tb * 256, 256)

        def rwh_t(k):
            return bfv(A6, O_RW + k * 8, 8)

        def rwl_t(k):
            return bfv(A6, O_RW + 128 + k * 8, 8)

        LN_STAGE = A3[:, 7168:7680]

        c.op("sy", lambda E: E.dma_start(out=IDT, in_=din["id128"].ap()),
             inc="dw", inck=16)
        c.op("sy", lambda E: E.dma_start(out=ONESC, in_=din["ones128"].ap()),
             inc="dw", inck=16)
        c.op("sy", lambda E: E.dma_start(out=A6[:, O_O64A:O_O64A + 1],
                                         in_=din["ones64a"].ap()),
             inc="dw", inck=16)
        c.op("sy", lambda E: E.dma_start(out=A6[:, O_O64B:O_O64B + 1],
                                         in_=din["ones64b"].ap()),
             inc="dw", inck=16)
        c.op("sy", lambda E: E.dma_start(out=ONR, in_=din["onesr"].ap()),
             inc="dw", inck=16)
        for i, nm in enumerate(("lq1", "lq2", "lk1", "lk2")):
            c.op("sy", lambda E, i=i, nm=nm: E.dma_start(
                out=A6[:, O_LQ + i:O_LQ + i + 1], in_=din[nm].ap()),
                inc="dw", inck=16)
        c.op("sy", lambda E: E.dma_start(
            out=A6[:, O_EPS1:O_EPS1 + 1], in_=din["epsln"].ap()),
             inc="dw", inck=16)
        c.op("sy", lambda E: E.dma_start(
            out=A6[:, O_EPS2:O_EPS2 + 1], in_=din["epsrms"].ap()),
             inc="dw", inck=16)
        c.op("sy", lambda E: E.dma_start(
            out=ap3(A1, 0, [[16384, 128], [16, DC], [1, NE]]),
            in_=ap3(din["rwT"], 0, [[NE, 128], [128 * NE, DC], [1, NE]])),
             inc="dw", inck=16)
        c.op("sy", lambda E: E.dma_start(
            out=ap3(A2, 0, [[16384, 128], [TC, DC], [1, TC]]),
            in_=ap3(din["xT"], 0, [[TC, 128], [128 * TC, DC], [1, TC]])),
             inc="dw", inck=16)

        c.wait("te", "dw")
        c.op("te", lambda E: E.matmul(B[6][:1, 0:1], A6[:, O_LQ:O_LQ + 1],
                                      A6[:, O_LQ + 2:O_LQ + 3],
                                      start=True, stop=True))
        c.op("te", lambda E: E.matmul(B[7][:1, 0:1], A6[:, O_LQ + 1:O_LQ + 2],
                                      A6[:, O_LQ + 3:O_LQ + 4],
                                      start=True, stop=True), inc="te")
        c.wait("se", "te")
        c.op("se", lambda E: E.activation(A6[:1, O_SML:O_SML + 1],
                                          B[6][:1, 0:1], AF.Exp))
        c.op("se", lambda E: E.activation(A6[:1, O_SML + 1:O_SML + 2],
                                          B[7][:1, 0:1], AF.Exp), inc="se")
        c.wait("ve", "se")
        c.op("ve", lambda E: E.tensor_sub(A6[:1, O_LAM:O_LAM + 1],
                                          A6[:1, O_SML:O_SML + 1],
                                          A6[:1, O_SML + 1:O_SML + 2]),
             drain=True)
        c.op("ve", lambda E: E.tensor_scalar(A6[:1, O_LAM:O_LAM + 1],
                                             A6[:1, O_LAM:O_LAM + 1],
                                             LAMBDA_INIT, None, ADD),
             inc="ve", drain=True)
        c.wait("ve", "dw")
        c.op("ve", lambda E: E.tensor_copy(bfv(A6, O_RW, 128),
                                           A1[:, 0:256]), drain=True)
        c.op("ve", lambda E: E.tensor_sub(bfv(A6, O_RW + 128, 128),
                                          A1[:, 0:256],
                                          bfv(A6, O_RW, 128)),
             inc="ve", drain=True)

        def ln_block(src, dst, tw, split=None):
            c.wait("te", "ve")
            c.wait("te", "dw")
            c.wait("se", "ve")
            c.wait("se", "te")
            c.wait("se", "dw")
            for k in range(DC):
                c.op("te", lambda E, k=k: E.matmul(
                    B[5][:1, 0:tw], ONESC, src(k),
                    start=(k == 0), stop=(k == DC - 1)),
                    inc="te" if k == DC - 1 else None)
            te_sq = {}
            for k in range(DC):
                sl = SQ0 if k % 2 == 0 else SQ1
                if k >= 2:
                    c.wait("se", "te", te_sq[k - 2])
                c.op("se", lambda E, k=k, sl=sl: E.activation(
                    A3[:, sl:sl + tw], src(k), AF.Square), inc="se")
                se_k = c.cnt["se"]
                c.wait("te", "se", se_k)
                te_sq[k] = c.op("te", lambda E, k=k, sl=sl: E.matmul(
                    B[6][:1, 0:tw], ONESC, A3[:, sl:sl + tw],
                    start=(k == 0), stop=(k == DC - 1)), inc="te")
            c.wait("ve", "te")
            c.op("ve", lambda E: E.tensor_scalar(
                A6[:1, O_ROW:O_ROW + tw], B[5][:1, 0:tw], 1.0 / D, None,
                MUL), drain=True)
            c.op("ve", lambda E: E.tensor_scalar(
                A6[:1, O_ROW2:O_ROW2 + tw], B[6][:1, 0:tw], 1.0 / D, None,
                MUL), drain=True)
            c.op("ve", lambda E: E.tensor_mul(
                B[6][:1, 0:tw], A6[:1, O_ROW:O_ROW + tw],
                A6[:1, O_ROW:O_ROW + tw]), drain=True)
            c.op("ve", lambda E: E.tensor_sub(
                A6[:1, O_ROW2:O_ROW2 + tw], A6[:1, O_ROW2:O_ROW2 + tw],
                B[6][:1, 0:tw]), inc="ve", drain=True)
            c.wait("se", "ve")
            c.op("se", lambda E: E.activation(
                A6[:1, O_ROW2:O_ROW2 + tw], A6[:1, O_ROW2:O_ROW2 + tw],
                AF.Sqrt, bias=A6[:1, O_EPS1:O_EPS1 + 1]), inc="se")
            c.wait("ve", "se")
            c.op("ve", lambda E: E.reciprocal(
                A6[:1, O_ROW2:O_ROW2 + tw], A6[:1, O_ROW2:O_ROW2 + tw]),
                drain=True)
            c.op("ve", lambda E: E.tensor_mul(
                A6[:1, O_ROW:O_ROW + tw], A6[:1, O_ROW:O_ROW + tw],
                A6[:1, O_ROW2:O_ROW2 + tw]), inc="ve", drain=True)
            c.wait("te", "ve")
            c.op("te", lambda E: E.matmul(B[4][:, 0:tw], ONR,
                                          A6[:1, O_ROW2:O_ROW2 + tw],
                                          start=True, stop=True))
            c.op("te", lambda E: E.matmul(B[5][:, 0:tw], ONR,
                                          A6[:1, O_ROW:O_ROW + tw],
                                          start=True, stop=True), inc="te")
            c.wait("ve", "te")
            if split is None:
                for k in range(DC):
                    c.op("ve", lambda E, k=k: E.tensor_mul(dst(k), src(k),
                                                           B[4][:, 0:tw]))
                    c.op("ve", lambda E, k=k: E.tensor_sub(dst(k), dst(k),
                                                           B[5][:, 0:tw]),
                         inc="ve" if k == DC - 1 else None)
            else:
                stage, xh_fn, xl_fn = split
                for k in range(DC):
                    c.op("ve", lambda E, k=k: E.tensor_mul(stage, src(k),
                                                           B[4][:, 0:tw]))
                    c.op("ve", lambda E, k=k: E.tensor_sub(stage, stage,
                                                           B[5][:, 0:tw]),
                         drain=True)
                    c.op("ve", lambda E, k=k: E.tensor_copy(xh_fn(k), stage),
                         drain=True)
                    c.op("ve", lambda E, k=k: E.tensor_sub(xl_fn(k), stage,
                                                           xh_fn(k)),
                         inc="ve" if k == DC - 1 else None, drain=True)

        def router(xh, xl):
            def vop(fn, inc=None, inck=1, drain=True):
                return c.op("ve", fn, inc=inc, inck=inck, drain=True)
            LG, LGT = 0, 512
            PP, RR, SEL, KP = 576, 592, 608, 624
            MM_, NM, SM4, IV = 640, 641, 642, 643
            CMT, CST = 648, 664
            for tb in range(2):
                c.wait("te", "ve")
                c.wait("te", "dw")
                for k in range(DC):
                    for p, (wv, xv) in enumerate((
                            (rwh_t(k), xh(k, tb)), (rwh_t(k), xl(k, tb)),
                            (rwl_t(k), xh(k, tb)))):
                        c.op("te", lambda E, wv=wv, xv=xv, k=k, p=p:
                             E.matmul(
                                 B[6][:16, 0:512], wv, xv,
                                 start=(k == 0 and p == 0),
                                 stop=(k == DC - 1 and p == 2)),
                             inc="te" if (k == DC - 1 and p == 2) else None)
                c.wait("ve", "te")
                vop( lambda E: E.tensor_copy(A5[:16, LG:LG + 512],
                                                   B[6][:16, 0:512]),
                     inc="ve", drain=True)
                for j in range(4):
                    c.wait("te", "ve")
                    c.op("te", lambda E, j=j: E.matmul(
                        B[7][:, 0:16],
                        A5[:16, LG + j * 128:LG + j * 128 + 128],
                        IDT[:16, 0:16], is_transpose=True,
                        start=True, stop=True), inc="te")
                    c.wait("ve", "te")
                    vop( lambda E, j=j: E.tensor_copy(
                        A5[:, LGT + j * 16:LGT + j * 16 + 16], B[7][:, 0:16]),
                        inc="ve", drain=True)
                for j in range(4):
                    vop( lambda E, j=j: E.reduce_max(
                        A5[:, MM_:MM_ + 1],
                        A5[:, LGT + j * 16:LGT + j * 16 + 16], axis=AX))
                    vop( lambda E: E.tensor_scalar(
                        A5[:, NM:NM + 1], A5[:, MM_:MM_ + 1], -1.0, None,
                        MUL), inc="ve", drain=True)
                    c.wait("se", "ve")
                    c.op("se", lambda E, j=j: E.activation(
                        A5[:, PP:PP + 16],
                        A5[:, LGT + j * 16:LGT + j * 16 + 16], AF.Exp,
                        bias=A5[:, NM:NM + 1],
                        accum_out=A5[:, IV:IV + 1]), inc="se", drain=True)
                    c.wait("ve", "se")
                    vop( lambda E: E.reciprocal(A5[:, MM_:MM_ + 1],
                                                      A5[:, IV:IV + 1]))
                    vop( lambda E: E.tensor_scalar(
                        A5[:, PP:PP + 16], A5[:, PP:PP + 16],
                        A5[:, MM_:MM_ + 1], None, MUL))
                    vop( lambda E: E.tensor_copy(A5[:, RR:RR + 16],
                                                       A5[:, PP:PP + 16]))
                    for r in range(4):
                        vop( lambda E: E.reduce_max(
                            A5[:, MM_:MM_ + 1], A5[:, RR:RR + 16], axis=AX))
                        vop( lambda E: E.tensor_scalar(
                            A5[:, KP:KP + 16], A5[:, RR:RR + 16],
                            A5[:, MM_:MM_ + 1], None, AluOpType.is_ge))
                        if r == 0:
                            vop( lambda E: E.tensor_copy(
                                A5[:, SEL:SEL + 16], A5[:, KP:KP + 16]))
                            vop( lambda E: E.tensor_copy(
                                A5[:, SM4:SM4 + 1], A5[:, MM_:MM_ + 1]))
                        else:
                            vop( lambda E: E.tensor_add(
                                A5[:, SEL:SEL + 16], A5[:, SEL:SEL + 16],
                                A5[:, KP:KP + 16]))
                            vop( lambda E: E.tensor_add(
                                A5[:, SM4:SM4 + 1], A5[:, SM4:SM4 + 1],
                                A5[:, MM_:MM_ + 1]))
                        if r < 3:
                            vop( lambda E: E.tensor_scalar(
                                A5[:, KP:KP + 16], A5[:, KP:KP + 16],
                                -1.0, 1.0, MUL, ADD))
                            vop( lambda E: E.tensor_mul(
                                A5[:, RR:RR + 16], A5[:, RR:RR + 16],
                                A5[:, KP:KP + 16]))
                    vop( lambda E: E.reciprocal(A5[:, IV:IV + 1],
                                                      A5[:, SM4:SM4 + 1]))
                    vop( lambda E: E.tensor_mul(
                        A5[:, CMT:CMT + 16], A5[:, PP:PP + 16],
                        A5[:, SEL:SEL + 16]))
                    vop( lambda E: E.tensor_scalar(
                        A5[:, CMT:CMT + 16], A5[:, CMT:CMT + 16],
                        A5[:, IV:IV + 1], None, MUL), inc="ve", drain=True)
                    c.wait("te", "ve")
                    c.op("te", lambda E: E.matmul(
                        B[7][:16, 0:128], A5[:, CMT:CMT + 16], IDT,
                        is_transpose=True, start=True, stop=True), inc="te")
                    c.wait("ve", "te")
                    vop( lambda E, j=j, tb=tb: E.tensor_copy(
                        A2[:16, tb * 512 + j * 128:tb * 512 + j * 128 + 128],
                        B[7][:16, 0:128]), inc="ve", drain=True)
            c.wait("gp", "ve")
            c.op("gp", lambda E: E.dma_start(out=cmbD.ap(),
                                             in_=A2[:16, 0:1024]),
                 inc="dg", inck=16)
            return c.cnt["dg"]

        def ffn_experts(xh, xl, cmb_dg, tap0=None):
            c.wait("sy", "dg", cmb_dg)
            c.wait("sy", "te")
            c.wait("sy", "ve")
            c.wait("ve", "te")
            c.wait("ve", "se")
            c.wait("ve", "dg", cmb_dg)
            c.op("ve", lambda E: E.memset(A2[:], 0.0), inc="ve")
            slot_free = [0, 0]
            split_done = [0, 0]
            stage_split = [0]
            dslot_free = [0, 0]
            dsplit_done = [0, 0]
            prev_cb = 0
            for e in range(FFN_NE):
                c.wait("sy", "te", prev_cb)
                c.op("sy", lambda E, e=e: E.dma_start(
                    out=A6[:1, O_CROW:O_CROW + TC], in_=cmbD.ap()[e]),
                    inc="cr", inck=16)
                c.wait("te", "cr")
                c.wait("te", "ve")
                for tb in range(2):
                    c.op("te", lambda E, tb=tb: E.matmul(
                        B[4 + tb][:, 0:512], ONR,
                        A6[:1, O_CROW + tb * 512:O_CROW + tb * 512 + 512],
                        start=True, stop=True), inc="te")
                prev_cb = c.cnt["te"]

                loads = []
                for j in range(FC):
                    loads.append(j * 128)
                    loads.append(FI + j * 128)
                ld_done = [None] * len(loads)

                def issue_gu_load(li, e=e, loads=loads, ld_done=ld_done):
                    off = loads[li]
                    sl = li % 2
                    base = 2048 + sl * 2048
                    c.wait("sy", "ve", stage_split[0])
                    c.op("sy", lambda E, e=e, off=off: E.dma_start(
                        out=ap3(A4, 0, [[6144, 128], [128, DC], [1, 128]]),
                        in_=ap3(din["gu"], e * D * 2 * FI + off,
                                [[2 * FI, 128], [128 * 2 * FI, DC],
                                 [1, 128]])), inc="g0", inck=16)
                    c.wait("ve", "g0")
                    c.wait("ve", "te", slot_free[sl])
                    c.op("ve", lambda E, base=base: E.tensor_copy(
                        bfv(A4, base, 1024), A4[:, 0:2048]), drain=True)
                    c.op("ve", lambda E, base=base: E.tensor_sub(
                        bfv(A4, base + 1024, 1024), A4[:, 0:2048],
                        bfv(A4, base, 1024)), inc="ve", drain=True)
                    stage_split[0] = c.cnt["ve"]
                    split_done[sl] = c.cnt["ve"]
                    ld_done[li] = ("ve", c.cnt["ve"])

                if not FFN_GATE:
                    continue
                for li in range(2):
                    issue_gu_load(li)

                def mm3(bank, sl, k, tb, start, stop):
                    base = 2048 + sl * 2048
                    wh = bfv(A4, base + k * 64, 64)
                    wl = bfv(A4, base + 1024 + k * 64, 64)
                    trips = ((wh, xh(k, tb)), (wh, xl(k, tb)),
                             (wl, xh(k, tb)))
                    for p, (wv, xv) in enumerate(trips):
                        st_ = start and p == 0
                        sp_ = stop and p == 2
                        c.op("te", lambda E, wv=wv, xv=xv, st_=st_, sp_=sp_,
                             bank=bank: E.matmul(
                                 B[bank][:, 0:512], wv, xv,
                                 start=st_, stop=sp_),
                             inc="te" if sp_ else None)

                for j in range(FC):
                    gsl = (2 * j) % 2
                    usl = (2 * j + 1) % 2
                    c.wait("te", *ld_done[2 * j])
                    c.wait("te", "se")
                    for tb in range(2):
                        for k in range(DC):
                            mm3(0 + tb, gsl, k, tb, k == 0, k == DC - 1)
                    gate_done = c.cnt["te"]
                    slot_free[gsl] = gate_done
                    if 2 * j + 2 < len(loads):
                        issue_gu_load(2 * j + 2)
                    c.wait("se", "te", gate_done)
                    c.wait("se", "ve")
                    for tb in range(2):
                        so = SILU0 if tb == 0 else SILU1
                        c.op("se", lambda E, tb=tb, so=so: E.activation(
                            A3[:, so:so + 512], B[0 + tb][:, 0:512],
                            AF.Silu), inc="se")
                    silu_done = c.cnt["se"]
                    c.wait("te", *ld_done[2 * j + 1])
                    c.wait("te", "ve")
                    for tb in range(2):
                        for k in range(DC):
                            mm3(2 + tb, usl, k, tb, k == 0, k == DC - 1)
                    up_done = c.cnt["te"]
                    slot_free[usl] = up_done
                    if 2 * j + 3 < len(loads):
                        issue_gu_load(2 * j + 3)
                    c.wait("ve", "se", silu_done)
                    c.wait("ve", "te", up_done)
                    for tb in range(2):
                        hst = 7680
                        so = SILU0 if tb == 0 else SILU1
                        hh = bfv(A3, j * 512 + tb * 256, 256)
                        hl = bfv(A3, 3072 + j * 512 + tb * 256, 256)
                        c.op("ve", lambda E, tb=tb, so=so, hst=hst:
                             E.scalar_tensor_tensor(
                                 A3[:, hst:hst + 512], B[2 + tb][:, 0:512],
                                 1.0, A3[:, so:so + 512], MUL, MUL),
                             drain=True)
                        c.op("ve", lambda E, tb=tb, hst=hst: E.tensor_mul(
                            A3[:, hst:hst + 512], A3[:, hst:hst + 512],
                            B[4 + tb][:, 0:512]), drain=True)
                        c.op("ve", lambda E, hh=hh, hst=hst: E.tensor_copy(
                            hh, A3[:, hst:hst + 512]), drain=True)
                        c.op("ve", lambda E, hh=hh, hl=hl, hst=hst:
                             E.tensor_sub(hl, A3[:, hst:hst + 512], hh),
                             inc="ve", drain=True)
                h_done = c.cnt["ve"]

                def issue_down(dd, e=e):
                    sl = dd % 2
                    c.wait("sy", "ve", stage_split[0])
                    c.op("sy", lambda E, e=e, dd=dd: E.dma_start(
                        out=ap3(A4, 0, [[6144, 128], [128, FC], [1, 128]]),
                        in_=ap3(din["dwn"], e * FI * D + dd * 128,
                                [[D, 128], [128 * D, FC], [1, 128]])),
                        inc="g0", inck=16)
                    c.wait("ve", "g0")
                    c.wait("ve", "te", dslot_free[sl])
                    b = sl * 768
                    c.op("ve", lambda E, b=b: E.tensor_copy(
                        bfv(A5, b, 384), A4[:, 0:768]), drain=True)
                    c.op("ve", lambda E, b=b: E.tensor_sub(
                        bfv(A5, b + 384, 384), A4[:, 0:768],
                        bfv(A5, b, 384)), inc="ve", drain=True)
                    stage_split[0] = c.cnt["ve"]
                    dsplit_done[sl] = c.cnt["ve"]

                if FFN_DOWN >= 1:
                    issue_down(0)
                    issue_down(1)
                for dd in range(DC if FFN_DOWN >= 1 else 0):
                    sl = dd % 2
                    if FFN_DOWN < 2:
                        if dd + 2 < DC:
                            issue_down(dd + 2)
                        continue
                    c.wait("te", "ve", dsplit_done[sl])
                    if dd == 0:
                        c.wait("te", "ve", h_done)
                    c.wait("te", "ve")
                    for tb in range(2):
                        bk = 6 + tb
                        for f in range(FC):
                            dwh = bfv(A5, sl * 768 + f * 64, 64)
                            dwl = bfv(A5, sl * 768 + 384 + f * 64, 64)
                            hh = bfv(A3, f * 512 + tb * 256, 256)
                            hl = bfv(A3, 3072 + f * 512 + tb * 256, 256)
                            trips = ((dwh, hh), (dwh, hl), (dwl, hh))
                            for p, (wv, xv) in enumerate(trips):
                                st_ = (f == 0 and p == 0)
                                sp_ = (f == FC - 1 and p == 2)
                                c.op("te", lambda E, wv=wv, xv=xv, st_=st_,
                                     sp_=sp_, bk=bk: E.matmul(
                                         B[bk][:, 0:512], wv, xv,
                                         start=st_, stop=sp_),
                                     inc="te" if sp_ else None)
                        mm_dn = c.cnt["te"]
                        if FFN_DOWN >= 3:
                            c.wait("ve", "te", mm_dn)
                            ac = dd * 1024 + tb * 512
                            c.op("ve", lambda E, ac=ac, bk=bk: E.tensor_add(
                                A2[:, ac:ac + 512], A2[:, ac:ac + 512],
                                B[bk][:, 0:512]), inc="ve")
                    dslot_free[sl] = c.cnt["te"]
                    if dd + 2 < DC:
                        issue_down(dd + 2)
                if e == 0 and tap0 is not None:
                    c.wait("gp", "ve")
                    c.op("gp", lambda E: E.dma_start(
                        out=ap3(tap0, 0, [[TC, 128], [128 * TC, DC],
                                          [1, TC]]),
                        in_=ap3(A2, 0, [[16384, 128], [TC, DC], [1, TC]])),
                        inc="dg", inck=16)
                    c.wait("ve", "dg", c.cnt["dg"])

        def residual_out(base_dram, dst_dram):
            c.wait("gp", "dg")
            for half in range(2):
                c.wait("gp", "ve")
                c.wait("gp", "te")
                c.op("gp", lambda E, half=half: E.dma_start(
                    out=ap3(A1, half * 8192,
                            [[16384, 128], [TC, 8], [1, TC]]),
                    in_=ap3(base_dram, half * 8 * 128 * TC,
                            [[TC, 128], [128 * TC, 8], [1, TC]])),
                    inc="dg", inck=16)
                ld = c.cnt["dg"]
                c.wait("ve", "dg", ld)
                c.wait("ve", "te")
                for kk in range(8):
                    k = half * 8 + kk
                    ao = half * 8192 + kk * TC
                    c.op("ve", lambda E, k=k, ao=ao:
                         E.scalar_tensor_tensor(
                             A1[:, ao:ao + TC], A2[:, k * TC:k * TC + TC],
                             0.5, A1[:, ao:ao + TC], MUL, ADD), inc="ve")
            c.wait("gp", "ve")
            c.op("gp", lambda E: E.dma_start(
                out=ap3(dst_dram, 0, [[TC, 128], [128 * TC, DC], [1, TC]]),
                in_=ap3(A1, 0, [[16384, 128], [TC, DC], [1, TC]])),
                inc="dg", inck=16)

        def emit_av(kt, se_exp, te_av):
            p = kt % 2
            c.wait("te", "se", se_exp[kt])
            st, sp = (kt == 0), (kt == 31)
            e1 = EOF + p * 512
            e2 = EOF + (2 + p) * 512
            c.op("te", lambda E, kt=kt, e1=e1, st=st, sp=sp: E.matmul(
                B[4][:, 0:512], A4[:, kt * 128:kt * 128 + 128],
                A2[:, e1:e1 + 512], start=st, stop=sp))
            c.op("te", lambda E, kt=kt, e2=e2, st=st, sp=sp: E.matmul(
                B[5][:, 0:512], A4[:, kt * 128:kt * 128 + 128],
                A2[:, e2:e2 + 512], start=st, stop=sp))
            c.op("te", lambda E, e1=e1, st=st, sp=sp: E.matmul(
                B[6][:1, 0:512], ONESC, A2[:, e1:e1 + 512],
                start=st, stop=sp))
            c.op("te", lambda E, e2=e2, st=st, sp=sp: E.matmul(
                B[7][:1, 0:512], ONESC, A2[:, e2:e2 + 512],
                start=st, stop=sp), inc="te")
            te_av[kt] = c.cnt["te"]

        def _schedule():
            if PL < 1:
                raise _PhaseStop()
            for tb in range(2):
                ln_block(lambda k, tb=tb: A2[:, k * TC + tb * 512:
                                             k * TC + tb * 512 + 512],
                         None, 512,
                         split=(LN_STAGE,
                                (lambda k, tb=tb: xh_t(k, tb)),
                                (lambda k, tb=tb: xl_t(k, tb))))
            if PL < 2:
                raise _PhaseStop()
            cmb_dg = router(xh_t, xl_t)
            if DEBUG_TAPS:
                c.wait("gp", "ve")
                c.wait("gp", "te")
                c.op("gp", lambda E: E.dma_start(
                    out=ap3(dbg["dbg_xa"], 0,
                            [[TC, 128], [128 * TC, DC], [1, TC]]),
                    in_=ap3(A1, 0, [[16384, 128], [TC, DC], [1, TC]])),
                    inc="dg", inck=16)
                tap_xa_dg = c.cnt["dg"]
                c.wait("te", "dg", tap_xa_dg)
            if PL < 3:
                raise _PhaseStop()
            ffn_experts(xh_t, xl_t, cmb_dg, tap0=dbg.get("dbg_acc0"))
            if DEBUG_TAPS:
                c.wait("gp", "ve")
                c.op("gp", lambda E: E.dma_start(out=dbg["dbg_cmb"].ap(),
                                                 in_=cmbD.ap()),
                     inc="dg", inck=16)
                c.op("gp", lambda E: E.dma_start(
                    out=ap3(dbg["dbg_acc"], 0,
                            [[TC, 128], [128 * TC, DC], [1, TC]]),
                    in_=ap3(A2, 0, [[16384, 128], [TC, DC], [1, TC]])),
                    inc="dg", inck=16)
                c.op("gp", lambda E: E.dma_start(
                    out=ap3(dbg["dbg_xln1"], 0,
                            [[TC, 128], [128 * TC, DC], [1, TC]]),
                    in_=ap3(A1, 0, [[16384, 128], [TC, DC], [1, TC]])),
                    inc="dg", inck=16)
                c.op("gp", lambda E: E.dma_start(
                    out=ap3(dbg["dbg_h"], 0, [[6144, 128], [1, 6144]]),
                    in_=ap3(A3, 0, [[8192, 128], [1, 6144]])),
                    inc="dg", inck=16)
            if PL < 4:
                raise _PhaseStop()
            residual_out(din["xT"], y1b)
            y1_st = c.cnt["dg"]

            if PL < 5:
                raise _PhaseStop()
            c.wait("gp", "dg", y1_st)
            c.op("gp", lambda E: E.collective_compute(
                "AllGather", mybir.AluOpType.bypass,
                replica_groups=[[0, 1, 2, 3], [4, 5, 6, 7]],
                ins=[y1b.ap().opt()], outs=[xg.ap().opt()]), inc="cc")

            if PL < 6:
                raise _PhaseStop()
            c.wait("sy", "cc")
            st_done = [0, 0]
            for blk in range(8):
                r, hf = blk // 2, blk % 2
                sl = blk % 2
                c.wait("sy", "dg", st_done[sl])
                c.wait("sy", "ve")
                c.wait("sy", "te")
                c.op("sy", lambda E, r=r, hf=hf, sl=sl: E.dma_start(
                    out=ap3(A1, sl * 8192, [[16384, 128], [512, DC], [1, 512]]),
                    in_=ap3(xg, r * D * TC + hf * 512,
                            [[TC, 128], [128 * TC, DC], [1, 512]])),
                    inc="dw", inck=16)
                ld = c.cnt["dw"]
                c.wait("ve", "dw", ld)
                ln_block(lambda k, sl=sl: A1[:, sl * 8192 + k * 512:
                                             sl * 8192 + k * 512 + 512],
                         lambda k, sl=sl: A1[:, sl * 8192 + k * 512:
                                             sl * 8192 + k * 512 + 512], 512)
                c.wait("gp", "ve")
                c.op("gp", lambda E, r=r, hf=hf, sl=sl: E.dma_start(
                    out=ap3(xlnD, r * D * TC + hf * 512,
                            [[TC, 128], [128 * TC, DC], [1, 512]]),
                    in_=ap3(A1, sl * 8192, [[16384, 128], [512, DC], [1, 512]])),
                    inc="dg", inck=16)
                st_done[sl] = c.cnt["dg"]
            ln2_st = c.cnt["dg"]

            if PL < 7:
                raise _PhaseStop()
            for hh in range(min(4, PL - 6)):
                c.wait("sy", "te")
                c.wait("sy", "ve")
                c.wait("sy", "se")
                if hh == 0:
                    c.wait("sy", "dg", ln2_st)
                    c.wait("ve", "dg", ln2_st)
                wdsts = (ap3(A4, 4096, [[6144, 128], [128, DC], [1, 128]]),
                         ap3(A3, 4096, [[8192, 128], [128, DC], [1, 128]]),
                         ap3(A3, 6144, [[8192, 128], [128, DC], [1, 128]]))
                for nm, dst in zip(("wq4", "wk4", "wv4"), wdsts):
                    c.op("sy", lambda E, nm=nm, dst=dst, hh=hh: E.dma_start(
                        out=dst,
                        in_=ap3(din[nm], hh * 128,
                                [[512, 128], [128 * 512, DC], [1, 128]])),
                        inc="dw", inck=16)
                w_ld = c.cnt["dw"]

                tile_free = [0, 0]
                for tbt in range(32):
                    r, pos = tbt // 8, tbt % 8
                    sl = tbt % 2
                    c.wait("sy", "te", tile_free[sl])
                    c.op("sy", lambda E, r=r, pos=pos, sl=sl: E.dma_start(
                        out=ap3(A3, sl * 2048,
                                [[8192, 128], [128, DC], [1, 128]]),
                        in_=ap3(xlnD, r * D * TC + pos * 128,
                                [[TC, 128], [128 * TC, DC], [1, 128]])),
                        inc="dw", inck=16)
                    ld = c.cnt["dw"]
                    c.wait("te", "dw", max(ld, w_ld))
                    c.wait("te", "ve")
                    wsrc = ((A4, 4096), (A3, 4096), (A3, 6144))
                    for pi in range(3):
                        wh, wo_ = wsrc[pi]
                        for k in range(DC):
                            c.op("te", lambda E, pi=pi, k=k, sl=sl, wh=wh,
                                 wo_=wo_: E.matmul(
                                     B[pi][:, 0:128],
                                     wh[:, wo_ + k * 128:wo_ + k * 128 + 128],
                                     A3[:, sl * 2048 + k * 128:
                                        sl * 2048 + k * 128 + 128],
                                     start=(k == 0), stop=(k == DC - 1)),
                                 inc="te" if k == DC - 1 else None)
                    qkv_done = c.cnt["te"]
                    tile_free[sl] = qkv_done
                    c.wait("ve", "te", qkv_done)
                    c.op("ve", lambda E, tbt=tbt: E.tensor_copy(
                        A2[:, QOF + tbt * 128:QOF + tbt * 128 + 128],
                        B[0][:, 0:128]))
                    c.op("ve", lambda E, tbt=tbt: E.tensor_copy(
                        A2[:, KOF + tbt * 128:KOF + tbt * 128 + 128],
                        B[1][:, 0:128]))
                    c.op("ve", lambda E: E.tensor_copy(
                        A2[:, VST:VST + 128], B[2][:, 0:128]), inc="ve")
                    vs_done = c.cnt["ve"]
                    c.wait("te", "ve", vs_done)
                    c.op("te", lambda E: E.matmul(
                        B[3][:, 0:128], A2[:, VST:VST + 128], IDT,
                        is_transpose=True, start=True, stop=True), inc="te")
                    c.wait("ve", "te")
                    c.op("ve", lambda E, tbt=tbt: E.tensor_copy(
                        A4[:, tbt * 128:tbt * 128 + 128], B[3][:, 0:128]),
                        inc="ve")

                def bound(off, qn_off):
                    c.wait("se", "ve")
                    c.wait("se", "te")
                    c.op("se", lambda E, off=off: E.activation(
                        A3[:, 4096:8192], A2[:, off:off + 4096], AF.Square),
                        inc="se")
                    sq = c.cnt["se"]
                    c.wait("te", "se", sq)
                    for blk in range(8):
                        c.wait("te", "ve")
                        c.op("te", lambda E, blk=blk: E.matmul(
                            B[5][:1, 0:512], A6[:, O_O64A:O_O64A + 1],
                            A3[:, 4096 + blk * 512:4096 + blk * 512 + 512],
                            start=True, stop=True))
                        c.op("te", lambda E, blk=blk: E.matmul(
                            B[6][:1, 0:512], A6[:, O_O64B:O_O64B + 1],
                            A3[:, 4096 + blk * 512:4096 + blk * 512 + 512],
                            start=True, stop=True), inc="te")
                        mmv = c.cnt["te"]
                        c.wait("ve", "te", mmv)
                        for hi, bk in ((0, 5), (1, 6)):
                            qo = qn_off + hi
                            if blk == 0:
                                c.op("ve", lambda E, bk=bk, qo=qo: E.reduce_max(
                                    A6[:1, qo:qo + 1], B[bk][:1, 0:512],
                                    axis=AX))
                            else:
                                c.op("ve", lambda E, bk=bk: E.reduce_max(
                                    A6[:1, O_SML:O_SML + 1], B[bk][:1, 0:512],
                                    axis=AX))
                                c.op("ve", lambda E, qo=qo: E.tensor_max(
                                    A6[:1, qo:qo + 1], A6[:1, qo:qo + 1],
                                    A6[:1, O_SML:O_SML + 1]))
                        c.op("ve", lambda E: E.tensor_copy(
                            A6[:1, O_SML + 1:O_SML + 2],
                            A6[:1, O_SML:O_SML + 1]), inc="ve")

                bound(QOF, O_QN)
                bound(KOF, O_QN + 2)
                c.wait("se", "ve")
                c.op("se", lambda E: E.activation(
                    A6[:1, O_QN:O_QN + 4], A6[:1, O_QN:O_QN + 4], AF.Sqrt),
                    inc="se")
                c.wait("ve", "se")
                c.op("ve", lambda E: E.tensor_mul(
                    A6[:1, O_QN + 4:O_QN + 6], A6[:1, O_QN:O_QN + 2],
                    A6[:1, O_QN + 2:O_QN + 4]))
                c.op("ve", lambda E: E.tensor_scalar(
                    A6[:1, O_QN + 4:O_QN + 6], A6[:1, O_QN + 4:O_QN + 6],
                    -SCALE, None, MUL), inc="ve")
                c.wait("te", "ve")
                c.op("te", lambda E: E.matmul(
                    B[4][:, 0:1], ONR, A6[:1, O_QN + 4:O_QN + 5],
                    start=True, stop=True))
                c.op("te", lambda E: E.matmul(
                    B[5][:, 0:1], ONR, A6[:1, O_QN + 5:O_QN + 6],
                    start=True, stop=True), inc="te")
                c.wait("ve", "te")
                c.op("ve", lambda E: E.tensor_copy(A6[:, O_NC1:O_NC1 + 1],
                                                   B[4][:, 0:1]))
                c.op("ve", lambda E: E.tensor_copy(A6[:, O_NC2:O_NC2 + 1],
                                                   B[5][:, 0:1]), inc="ve")

                for qb in range(8):
                    se_exp = {}
                    te_av = {}
                    for kt in range(32):
                        p = kt % 2
                        if kt == 0:
                            c.wait("te", "ve")
                        if kt >= 2:
                            c.wait("te", "se", se_exp[kt - 2])
                        for half in range(2):
                            bnk = (0 if half == 0 else 2) + p
                            po = half * 64
                            c.op("te", lambda E, kt=kt, po=po, bnk=bnk, qb=qb:
                                 E.matmul(
                                     B[bnk][:, 0:512],
                                     A2[po:po + 64,
                                        KOF + kt * 128:KOF + kt * 128 + 128],
                                     A2[po:po + 64,
                                        QOF + qb * 512:QOF + qb * 512 + 512],
                                     start=True, stop=True),
                                 inc="te" if half == 1 else None)
                        te_scr = c.cnt["te"]
                        if kt >= 1:
                            emit_av(kt - 1, se_exp, te_av)
                        c.wait("se", "te", te_scr)
                        if kt >= 2:
                            c.wait("se", "te", te_av.get(kt - 2, 0))
                        for half in range(2):
                            bnk = (0 if half == 0 else 2) + p
                            nco = O_NC1 if half == 0 else O_NC2
                            eo = EOF + (half * 2 + p) * 512
                            c.op("se", lambda E, bnk=bnk, nco=nco, eo=eo:
                                 E.activation(
                                     A2[:, eo:eo + 512], B[bnk][:, 0:512],
                                     AF.Exp, bias=A6[:, nco:nco + 1],
                                     scale=SCALE),
                                 inc="se" if half == 1 else None)
                        se_exp[kt] = c.cnt["se"]
                    emit_av(31, se_exp, te_av)
                    c.wait("ve", "te", te_av[31])
                    c.op("ve", lambda E: E.reciprocal(A6[:1, O_ROW:O_ROW + 512],
                                                      B[6][:1, 0:512]))
                    c.op("ve", lambda E: E.reciprocal(
                        A6[:1, O_ROW2:O_ROW2 + 512], B[7][:1, 0:512]))
                    c.op("ve", lambda E: E.tensor_scalar(
                        A6[:1, O_ROW2:O_ROW2 + 512],
                        A6[:1, O_ROW2:O_ROW2 + 512],
                        A6[:1, O_LAM:O_LAM + 1], None, MUL), inc="ve")
                    c.wait("te", "ve")
                    c.op("te", lambda E: E.matmul(
                        B[0][:, 0:512], ONR, A6[:1, O_ROW:O_ROW + 512],
                        start=True, stop=True))
                    c.op("te", lambda E: E.matmul(
                        B[1][:, 0:512], ONR, A6[:1, O_ROW2:O_ROW2 + 512],
                        start=True, stop=True), inc="te")
                    c.wait("ve", "te")
                    c.op("ve", lambda E: E.tensor_copy(A3[:, 4096:4608],
                                                       B[4][:, 0:512]))
                    c.op("ve", lambda E: E.tensor_copy(A3[:, 4608:5120],
                                                       B[5][:, 0:512]))
                    c.op("ve", lambda E: E.tensor_mul(A3[:, 4096:4608],
                                                      A3[:, 4096:4608],
                                                      B[0][:, 0:512]))
                    c.op("ve", lambda E: E.tensor_mul(A3[:, 4608:5120],
                                                      A3[:, 4608:5120],
                                                      B[1][:, 0:512]))
                    c.op("ve", lambda E, qb=qb: E.tensor_sub(
                        A2[:, OOF + qb * 512:OOF + qb * 512 + 512],
                        A3[:, 4096:4608], A3[:, 4608:5120]), inc="ve")

                c.wait("se", "ve")
                c.wait("se", "te")
                c.op("se", lambda E: E.activation(A3[:, 4096:8192],
                                                  A2[:, OOF:OOF + 4096],
                                                  AF.Square), inc="se")
                sq_rms = c.cnt["se"]
                c.wait("te", "se", sq_rms)
                for blk in range(8):
                    c.wait("te", "ve")
                    c.op("te", lambda E, blk=blk: E.matmul(
                        B[6][:1, 0:512], ONESC,
                        A3[:, 4096 + blk * 512:4096 + blk * 512 + 512],
                        start=True, stop=True), inc="te")
                    mm_s = c.cnt["te"]
                    c.wait("ve", "te", mm_s)
                    c.op("ve", lambda E: E.tensor_scalar(
                        A6[:1, O_ROW:O_ROW + 512], B[6][:1, 0:512], 1.0 / 128,
                        None, MUL), inc="ve")
                    r_ve = c.cnt["ve"]
                    c.wait("se", "ve", r_ve)
                    c.op("se", lambda E: E.activation(
                        A6[:1, O_ROW:O_ROW + 512], A6[:1, O_ROW:O_ROW + 512],
                        AF.Sqrt, bias=A6[:1, O_EPS2:O_EPS2 + 1]), inc="se")
                    c.wait("ve", "se")
                    c.op("ve", lambda E: E.reciprocal(
                        A6[:1, O_ROW:O_ROW + 512], A6[:1, O_ROW:O_ROW + 512]))
                    c.op("ve", lambda E: E.tensor_scalar(
                        A6[:1, O_ROW:O_ROW + 512], A6[:1, O_ROW:O_ROW + 512],
                        1.0 - LAMBDA_INIT, None, MUL), inc="ve")
                    c.wait("te", "ve")
                    c.op("te", lambda E: E.matmul(
                        B[4][:, 0:512], ONR, A6[:1, O_ROW:O_ROW + 512],
                        start=True, stop=True), inc="te")
                    c.wait("ve", "te")
                    c.op("ve", lambda E, blk=blk: E.tensor_mul(
                        A2[:, OOF + blk * 512:OOF + blk * 512 + 512],
                        A2[:, OOF + blk * 512:OOF + blk * 512 + 512],
                        B[4][:, 0:512]), inc="ve")

                for lp in range(32):
                    s_hi, lo = lp // 16, lp % 16
                    cpr = 2 * hh + s_hi
                    base = OOF + s_hi * 64 + 4 * lo
                    c.op("ve", lambda E, base=base: E.tensor_copy(
                        A2[:, VST:VST + 128],
                        ap3(A2, base, [[16384, 128], [1, 4], [1024, 4],
                                       [128, 8]])), inc="ve")
                    g_done = c.cnt["ve"]
                    c.wait("te", "ve", g_done)
                    bnk = 2 + lp % 2
                    c.op("te", lambda E, bnk=bnk: E.matmul(
                        B[bnk][:, 0:128], A2[:, VST:VST + 128], IDT,
                        is_transpose=True, start=True, stop=True), inc="te")
                    c.wait("ve", "te")
                    c.op("ve", lambda E, lo=lo, cpr=cpr, bnk=bnk: E.tensor_copy(
                        A1[:, lo * 1024 + cpr * 128:
                           lo * 1024 + cpr * 128 + 128], B[bnk][:, 0:128]),
                        inc="ve")

            if PL < 11:
                raise _PhaseStop()
            c.wait("sy", "te")
            c.wait("sy", "ve")
            c.wait("sy", "se")
            wo_free = [0, 0]
            res_free = [0, 0]
            for m in range(DC):
                wsl = m % 2
                wdst = (ap3(A4, 4096, [[6144, 128], [128, DC], [1, 128]])
                        if wsl == 0 else
                        ap3(A3, 4096, [[8192, 128], [128, DC], [1, 128]]))
                c.wait("sy", "te", wo_free[wsl])
                c.op("sy", lambda E, m=m, wdst=wdst: E.dma_start(
                    out=wdst,
                    in_=ap3(din["woT"], m * 128,
                            [[D, 128], [128 * D, DC], [1, 128]])),
                    inc="dw", inck=16)
                w_ld = c.cnt["dw"]
                rsl = m % 2
                c.wait("sy", "ve", res_free[rsl])
                c.op("sy", lambda E, m=m, rsl=rsl: E.dma_start(
                    out=ap3(A3, 6144 + rsl * 1024, [[8192, 128], [1, TC]]),
                    in_=ap3(y1b, m * 128 * TC, [[TC, 128], [1, TC]])),
                    inc="dw", inck=16)
                r_ld = c.cnt["dw"]
                for jb in range(2):
                    bnk = jb
                    c.wait("te", "dw", w_ld)
                    c.wait("te", "ve")
                    for lo in range(DC):
                        if wsl == 0:
                            wap = A4[:, 4096 + lo * 128:4096 + lo * 128 + 128]
                        else:
                            wap = A3[:, 4096 + lo * 128:4096 + lo * 128 + 128]
                        c.op("te", lambda E, lo=lo, jb=jb, bnk=bnk, wap=wap:
                             E.matmul(
                                 B[bnk][:, 0:512], wap,
                                 A1[:, lo * 1024 + jb * 512:
                                    lo * 1024 + jb * 512 + 512],
                                 start=(lo == 0), stop=(lo == DC - 1)),
                             inc="te" if lo == DC - 1 else None)
                    mm_done = c.cnt["te"]
                    c.wait("ve", "te", mm_done)
                    c.wait("ve", f"r{rsl}")
                    c.op("ve", lambda E, m=m, jb=jb, bnk=bnk, rsl=rsl:
                         E.scalar_tensor_tensor(
                             A2[:, m * TC + jb * 512:m * TC + jb * 512 + 512],
                             B[bnk][:, 0:512], 1.0,
                             A3[:, 6144 + rsl * 1024 + jb * 512:
                                6144 + rsl * 1024 + jb * 512 + 512],
                             MUL, ADD), inc="ve")
                wo_free[wsl] = c.cnt["te"]
                res_free[rsl] = c.cnt["ve"]
                c.wait("gp", "ve")
                c.op("gp", lambda E, m=m: E.dma_start(
                    out=ap3(x2b, m * 128 * TC, [[TC, 128], [1, TC]]),
                    in_=A2[:, m * TC:m * TC + TC]), inc="dg", inck=16)

            if PL < 12:
                raise _PhaseStop()
            for tb in range(2):
                ln_block(lambda k, tb=tb: A2[:, k * TC + tb * 512:
                                             k * TC + tb * 512 + 512],
                         None, 512,
                         split=(LN_STAGE,
                                (lambda k, tb=tb: xh_t(k, tb)),
                                (lambda k, tb=tb: xl_t(k, tb))))
            if PL < 13:
                raise _PhaseStop()
            cmb_dg2 = router(xh_t, xl_t)
            if PL < 14:
                raise _PhaseStop()
            ffn_experts(xh_t, xl_t, cmb_dg2)
            if PL < 15:
                raise _PhaseStop()
            residual_out(x2b, outT)

        try:
            _schedule()
        except _PhaseStop:
            c.wait("gp", "ve")
            c.wait("gp", "te")
            c.wait("gp", "se")
            c.wait("gp", "dw")
            c.op("gp", lambda E: E.dma_start(
                out=ap3(outT, 0, [[TC, 128], [128 * TC, DC], [1, TC]]),
                in_=ap3(A1, 0, [[16384, 128], [TC, DC], [1, TC]])),
                inc="dg", inck=16)

        _phase_stopped = False
        if DEBUG_TAPS:
            c.wait("gp", "dg")
            c.wait("gp", "cc")
            c.op("gp", lambda E: E.dma_start(out=dbg["dbg_y1"].ap(),
                                             in_=y1b.ap()),
                 inc="dg", inck=16)
            c.op("gp", lambda E: E.dma_start(out=dbg["dbg_xg"].ap(),
                                             in_=xg.ap()),
                 inc="dg", inck=16)
            c.op("gp", lambda E: E.dma_start(out=dbg["dbg_xln"].ap(),
                                             in_=xlnD.ap()),
                 inc="dg", inck=16)
            c.op("gp", lambda E: E.dma_start(out=dbg["dbg_x2"].ap(),
                                             in_=x2b.ap()),
                 inc="dg", inck=16)
        c.wait("gp", "dg")

        with nc.Block() as blk:
            @blk.tensor
            def _(E):
                for f in c.ops["te"]:
                    f(E)

            @blk.vector
            def _(E):
                for f in c.ops["ve"]:
                    f(E)

            @blk.scalar
            def _(E):
                for f in c.ops["se"]:
                    f(E)

            @blk.sync
            def _(E):
                for f in c.ops["sy"]:
                    f(E)

            @blk.gpsimd
            def _(E):
                for f in c.ops["gp"]:
                    f(E)

    return nc


def _mk_host_constants():
    ones64a = np.zeros((128, 1), np.float32)
    ones64a[:64] = 1.0
    ones64b = np.zeros((128, 1), np.float32)
    ones64b[64:] = 1.0
    return {
        "id128": np.eye(128, dtype=np.float32),
        "ones128": np.ones((128, 1), np.float32),
        "onesr": np.ones((1, 128), np.float32),
        "ones64a": ones64a,
        "ones64b": ones64b,
        "epsln": np.full((128, 1), LN_EPS, np.float32),
        "epsrms": np.full((128, 1), RMS_EPS, np.float32),
    }


def _token_perm(g):
    j = np.arange(TC)
    e = j % 128
    cpr = j // 128
    hprime = cpr // 2
    s_hi = cpr % 2
    return 32 * e + 8 * g + 2 * hprime + s_hi


def kernel(L0_attn_x, L0_attn_wq, L0_attn_wk, L0_attn_wv, L0_attn_wo,
           L0_attn_lambda_q1, L0_attn_lambda_q2, L0_attn_lambda_k1,
           L0_attn_lambda_k2, L0_ffn_router_weight, L0_ffn_gate_up_proj,
           L0_ffn_down_proj):
    x = np.asarray(L0_attn_x, np.float32)
    wq = np.asarray(L0_attn_wq, np.float32)
    wk = np.asarray(L0_attn_wk, np.float32)
    wv = np.asarray(L0_attn_wv, np.float32)
    wo = np.asarray(L0_attn_wo, np.float32)
    rw = np.asarray(L0_ffn_router_weight, np.float32)
    gup = np.ascontiguousarray(np.asarray(L0_ffn_gate_up_proj, np.float32))
    dwn = np.ascontiguousarray(np.asarray(L0_ffn_down_proj, np.float32))
    consts = _mk_host_constants()

    wqT = np.ascontiguousarray(wq.T)
    wkT = np.ascontiguousarray(wk.T)
    wvT = np.ascontiguousarray(wv.T)
    woT = np.ascontiguousarray(wo.T)
    rwT = np.ascontiguousarray(rw.T)

    def chunks(a, ncol):
        return np.ascontiguousarray(a.reshape(DC, 128, ncol))

    in_maps = []
    for core in range(8):
        bc, g = core // 4, core % 4
        tp = _token_perm(g)
        xTc = np.ascontiguousarray(x[bc].T[:, tp])
        m = {
            "xT": chunks(xTc, TC),
            "rwT": chunks(rwT, NE),
            "gu": gup,
            "dwn": dwn,
            "wq4": chunks(np.ascontiguousarray(
                wqT[:, 512 * g:512 * g + 512]), 512),
            "wk4": chunks(np.ascontiguousarray(
                wkT[:, 512 * g:512 * g + 512]), 512),
            "wv4": chunks(np.ascontiguousarray(
                wvT[:, 512 * g:512 * g + 512]), 512),
            "woT": chunks(woT, D),
            "lq1": np.asarray(L0_attn_lambda_q1, np.float32).reshape(128, 1),
            "lq2": np.asarray(L0_attn_lambda_q2, np.float32).reshape(128, 1),
            "lk1": np.asarray(L0_attn_lambda_k1, np.float32).reshape(128, 1),
            "lk2": np.asarray(L0_attn_lambda_k2, np.float32).reshape(128, 1),
        }
        m.update(consts)
        in_maps.append(m)

    nc = build_program()
    res = run_bass_kernel_spmd(nc, in_maps, core_ids=list(range(8)))

    out = np.empty((2, S, D), np.float32)
    for core in range(8):
        bc, g = core // 4, core % 4
        tp = _token_perm(g)
        oT = np.asarray(res.results[core]["outT"]).reshape(D, TC)
        out[bc, tp, :] = oT.T
    return out
